# version 23
# baseline (speedup 1.0000x reference)
import numpy as np
from contextlib import ExitStack

import concourse.bass as bass
import concourse.mybir as mybir
from concourse.bass_utils import run_bass_kernel_spmd
from concourse.ap import AP

F32 = mybir.dt.float32
BF16 = mybir.dt.bfloat16
F8 = mybir.dt.float8e4
AF = mybir.ActivationFunctionType
ALU = mybir.AluOpType

DEBUG_TAPS = False
S_TOT = 8192
NCORES = 8
S = S_TOT // NCORES
NB = S // 128
DM = 2048
NI = DM // 128
NKV = 4
NH = 16
F = 768
NF = F // 128
I_FFN = 6144
NIC = I_FFN // 128
EPS = 1e-8
LN_EPS = 1e-5
SEC = 2 * 256 * I_FFN + 768 * DM
QT = SEC // (128 * 512)


class Builder:
    def __init__(self):
        self.nc = bass.Bass("TRN2", target_bir_lowering=False, debug=False)
        self.Q = {"sync": [], "tensor": [], "vector": [], "scalar": [], "gpsimd": []}
        self.last_op_idx = {e: None for e in self.Q}
        self.ops_since_inc = {e: 0 for e in self.Q}
        self.last_inc_ev = {e: None for e in self.Q}
        self.st = ExitStack()
        nc = self.nc
        e = self.st.enter_context
        self.sems = {}
        for name in ("p_t", "p_v", "p_s", "p_g", "o_d", "cc"):
            self.sems[name] = [e(nc.semaphore(name)), 0]
        self.ENG_SEM = {"tensor": "p_t", "vector": "p_v", "scalar": "p_s",
                        "gpsimd": "p_g"}
        self.dma_pool = {"sync": [], "gpsimd": [], "scalar": []}
        for qn, n in (("sync", 16), ("gpsimd", 12), ("scalar", 8)):
            for i in range(n):
                nm = f"dq_{qn}_{i}"
                self.sems[nm] = [e(nc.semaphore(nm)), 0]
                self.dma_pool[qn].append(nm)
        self.pool_idx = {"sync": 0, "gpsimd": 0, "scalar": 0}
        self.cur_grp = {"sync": None, "gpsimd": None, "scalar": None}
        self.waited = {eng: {} for eng in self.Q}

    def raw(self, eng, fn):
        self.Q[eng].append(fn)

    def op(self, eng, fn, inc=False):
        if inc:
            s = self.sems[self.ENG_SEM[eng]]
            s[1] += 1
            h, v = s[0], s[1]
            self.Q[eng].append(lambda E, fn=fn, h=h: fn(E).then_inc(h, 1))
            self.last_op_idx[eng] = len(self.Q[eng]) - 1
            self.ops_since_inc[eng] = 0
            self.last_inc_ev[eng] = (self.ENG_SEM[eng], v)
            return self.last_inc_ev[eng]
        self.Q[eng].append(fn)
        self.last_op_idx[eng] = len(self.Q[eng]) - 1
        self.ops_since_inc[eng] += 1
        return None

    def op_cc(self, eng, fn):
        s = self.sems["cc"]
        s[1] += 1
        h, v = s[0], s[1]
        self.Q[eng].append(lambda E, fn=fn, h=h: fn(E).then_inc(h, 1))
        return ("cc", v)

    def fence_evs(self):
        evs = [(n, self.sems[n][1]) for n in ("p_t", "p_v", "p_s")]
        for qn in self.dma_pool:
            for nm in self.dma_pool[qn]:
                if self.sems[nm][1] > 0:
                    evs.append((nm, self.sems[nm][1]))
        return evs

    def mark(self, eng):
        if self.ops_since_inc[eng] == 0 and self.last_inc_ev[eng] is not None:
            return self.last_inc_ev[eng]
        i = self.last_op_idx[eng]
        assert i is not None, f"mark({eng}) with no prior datapath op"
        s = self.sems[self.ENG_SEM[eng]]
        s[1] += 1
        h, v = s[0], s[1]
        old = self.Q[eng][i]
        self.Q[eng][i] = lambda E, old=old, h=h: old(E).then_inc(h, 1)
        self.ops_since_inc[eng] = 0
        self.last_inc_ev[eng] = (self.ENG_SEM[eng], v)
        return self.last_inc_ev[eng]

    def sop(self, eng, fn):
        self.waitf(eng, self.last_inc_ev[eng])
        return self.op(eng, fn, inc=True)

    def dma(self, eng, out, in_, cont=False, **kw):
        if not cont or self.cur_grp[eng] is None:
            i = self.pool_idx[eng]
            pool = self.dma_pool[eng]
            self.pool_idx[eng] = (i + 1) % len(pool)
            nm = pool[i]
            st = self.sems[nm]
            if st[1] > 0:
                self.waitf(eng, (nm, st[1]))
            self.cur_grp[eng] = nm
        nm = self.cur_grp[eng]
        st = self.sems[nm]
        st[1] += 16
        h, v = st[0], st[1]
        self.Q[eng].append(lambda E, out=out, in_=in_, kw=kw, h=h:
                           E.dma_start(out=out, in_=in_, **kw).then_inc(h, 16))
        return (nm, v)

    def wait_all_dma(self, eng, queue):
        for nm in self.dma_pool[queue]:
            st = self.sems[nm]
            if st[1] > 0:
                self.waitf(eng, (nm, st[1]))

    def wait(self, eng, ev):
        if ev is None:
            return
        sem_name, val = ev
        if self.ENG_SEM.get(eng) == sem_name:
            return
        cur = self.waited[eng].get(sem_name, 0)
        if val <= cur:
            return
        self.waited[eng][sem_name] = val
        h = self.sems[sem_name][0]
        self.Q[eng].append(lambda E, h=h, v=val: E.wait_ge(h, v))

    def waitf(self, eng, ev):
        if ev is None:
            return
        sem_name, val = ev
        cur = self.waited[eng].get(sem_name, 0)
        if val <= cur:
            return
        self.waited[eng][sem_name] = val
        h = self.sems[sem_name][0]
        self.Q[eng].append(lambda E, h=h, v=val: E.wait_ge(h, v))

    def join(self):
        evs = [self.mark(eng) for eng in ("tensor", "vector", "scalar", "gpsimd")]
        for qn in self.dma_pool:
            for nm in self.dma_pool[qn]:
                st = self.sems[nm]
                if st[1] > 0:
                    evs.append((nm, st[1]))
        for eng in self.Q:
            for ev in evs:
                self.waitf(eng, ev) if ev[0].startswith("dq_") else self.wait(eng, ev)

    class Buf:
        def __init__(self, bld, handle, name):
            self.b = bld
            self.h = handle
            self.name = name
            self.w_ev = None
            self.w_eng = None
            self.r_evs = []

        def rd(self, eng):
            self.b.waitf(eng, self.w_ev)

        def rdone(self, ev):
            if ev is not None:
                self.r_evs.append(ev)

        def wr(self, eng):
            self.b.waitf(eng, self.w_ev)
            for ev in self.r_evs:
                self.b.waitf(eng, ev)

        def wdone(self, ev, eng=None):
            self.w_ev = ev
            self.r_evs = []

        def seed(self, evs):
            self.r_evs.extend(evs)

    def buf(self, handle, name):
        return Builder.Buf(self, handle, name)


def build_nc():
    B = Builder()
    nc = B.nc
    e = B.st.enter_context
    op, dma, wait, mark = B.op, B.dma, B.wait, B.mark

    x_in = nc.dram_tensor("x", [S, DM], F32, kind="ExternalInput").ap()
    wqT = nc.dram_tensor("wqT", [NH, 128, NI * 128], F32, kind="ExternalInput").ap()
    wkT = nc.dram_tensor("wkT", [NKV, 128, NI * 128], F32, kind="ExternalInput").ap()
    wgT = nc.dram_tensor("wgT", [4, 128, NI * 512], F32, kind="ExternalInput").ap()
    wmbv_h = nc.dram_tensor("wmbv", [NKV, 128, NI * 640], F32,
                            kind="ExternalInput").ap()
    bq_h = nc.dram_tensor("bq", [DM], F32, kind="ExternalInput").ap()
    bk_h = nc.dram_tensor("bk", [512], F32, kind="ExternalInput").ap()
    bv_h = nc.dram_tensor("bv", [512], F32, kind="ExternalInput").ap()
    bg_h = nc.dram_tensor("bg", [DM], F32, kind="ExternalInput").ap()
    bmb_h = nc.dram_tensor("bmb", [DM], F32, kind="ExternalInput").ap()
    mn_pre = nc.dram_tensor("mn_pre", [NKV, 128, 2 * NF * 258], F32,
                            kind="ExternalInput").ap()
    mn_seq = nc.dram_tensor("mn_seq", [128, NF * 2064], F32,
                            kind="ExternalInput").ap()
    wsl_h = nc.dram_tensor("wslice", [SEC], F32, kind="ExternalInput").ap()
    gs_h = nc.dram_tensor("gate_scale", [I_FFN], F32, kind="ExternalInput").ap()
    us_h = nc.dram_tensor("up_scale", [I_FFN], F32, kind="ExternalInput").ap()
    ds_h = nc.dram_tensor("down_scale", [DM], F32, kind="ExternalInput").ap()
    out_h = nc.dram_tensor("out", [S, DM], F32, kind="ExternalOutput").ap()

    dbg_memh = (nc.dram_tensor("dbg_memh", [128, NB * DM], BF16).ap()
                if DEBUG_TAPS else None)
    dbg_x2 = (nc.dram_tensor("dbg_x2", [128, NB * DM], F32).ap()
              if DEBUG_TAPS else None)
    dbg_nm = (nc.dram_tensor("dbg_nm", [128, NF * NH * 129], BF16).ap()
              if DEBUG_TAPS else None)
    ard_in = nc.dram_tensor("ard_in", [F, NH * 128], BF16)
    ard_out = nc.dram_tensor("ard_out", [F, NH * 128], BF16)
    arn_in = nc.dram_tensor("arn_in", [NH, F], BF16)
    arn_out = nc.dram_tensor("arn_out", [NH, F], BF16)
    ag_in = nc.dram_tensor("ag_in", [SEC], F8)
    ag_out = nc.dram_tensor("ag_out", [NCORES * SEC], F8)

    banks = [e(nc.psum_tensor(f"bank{i}", [128, 512], F32)) for i in range(8)]
    bbuf = [B.buf(banks[i], f"bank{i}") for i in range(8)]

    x_tm = e(nc.sbuf_tensor("x_tm", [128, NB * DM], F32))
    lnXt = e(nc.sbuf_tensor("lnXt", [128, NI * S], BF16))
    ln_sb = e(nc.sbuf_tensor("ln_sb", [128, DM], BF16))
    sca = e(nc.sbuf_tensor("sca", [128, 288], F32))
    ident = e(nc.sbuf_tensor("ident", [128, 128], BF16))
    rollm = e(nc.sbuf_tensor("rollm", [128, 6 * 128], BF16))
    ones_col = e(nc.sbuf_tensor("ones_col", [128, 1], BF16))
    ones_row = e(nc.sbuf_tensor("ones_row", [1, 128], BF16))
    ones_f32 = e(nc.sbuf_tensor("ones_f32", [1, 1], F32))
    ones_bf = e(nc.sbuf_tensor("ones_bf", [1, 1], BF16))
    bkq = e(nc.sbuf_tensor("bkq", [128, 2 * (NKV + NH)], F32))
    coefb = e(nc.sbuf_tensor("coefb", [128, 32], BF16))
    cst = e(nc.sbuf_tensor("cst", [128, 2], F32))

    bx = B.buf(x_tm, "x_tm")
    bln = B.buf(lnXt, "lnXt")
    blnsb = [B.buf(ln_sb, "ln_sb")]
    bcoef = B.buf(coefb, "coefb")

    C_LM, C_LI, C_LMI, C_L1I, C_NMI = 0, 8, 16, 24, 32
    C_SSQ, C_N, C_EN, C_INVN, C_RSSQ = 40, 48, 56, 64, 72
    C_RK, C_DPV, C_TMUL, C_NC1, C_COEF = 80, 112, 144, 176, 208
    C_QSSQ, C_QEN, C_NAW = 80, 96, 112
    C_TMP = 256

    v0 = op("gpsimd", lambda g: g.memset(ident[:, :], 0.0), inc=True)
    B.waitf("gpsimd", v0)
    v_id = op("gpsimd", lambda g: g.affine_select(
        out=ident[:, :], in_=ident[:, :], compare_op=ALU.not_equal, fill=1.0,
        base=0, pattern=[[-1, 128]], channel_multiplier=1), inc=True)
    v0r = op("gpsimd", lambda g: g.memset(rollm[:, :], 0.0), inc=True)
    B.waitf("gpsimd", v0r)
    v_roll = None
    for j_ in (1, 2, 3):
        op("gpsimd", lambda g, j_=j_: g.affine_select(
            out=rollm[:, (j_ - 1) * 128:j_ * 128],
            in_=rollm[:, (j_ - 1) * 128:j_ * 128], compare_op=ALU.not_equal,
            fill=1.0, base=j_, pattern=[[-1, 128]], channel_multiplier=1))
        v_roll = op("gpsimd", lambda g, j_=j_: g.affine_select(
            out=rollm[:, (2 + j_) * 128:(3 + j_) * 128],
            in_=rollm[:, (2 + j_) * 128:(3 + j_) * 128],
            compare_op=ALU.not_equal, fill=1.0, base=j_ - 128,
            pattern=[[-1, 128]], channel_multiplier=1), inc=True)
        B.waitf("gpsimd", v_roll)
    op("gpsimd", lambda g: g.memset(cst[:, 0:1], float(LN_EPS)), inc=True)
    v_cst = op("gpsimd", lambda g: g.memset(cst[:, 1:2], float(EPS * EPS)), inc=True)
    op("gpsimd", lambda g: g.memset(ones_col[:, :], 1.0), inc=True)
    op("gpsimd", lambda g: g.memset(ones_row[:, :], 1.0), inc=True)
    op("gpsimd", lambda g: g.memset(ones_f32[:, :], 1.0), inc=True)
    v_ones = op("gpsimd", lambda g: g.memset(ones_bf[:, :], 1.0), inc=True)
    v_cst = v_ones
    v_id = ("p_g", v_ones[1]) if False else v_id

    dma("sync", bkq[:, 0:NKV], AP(bk_h.tensor, 0, [[1, 128], [128, NKV]]),
        allow_slow_non_contiguous=True)
    v_bkq0 = dma("sync", bkq[:, NKV:NKV + NH],
                 AP(bq_h.tensor, 0, [[1, 128], [128, NH]]),
                 cont=True, allow_slow_non_contiguous=True)
    wait("vector", v_bkq0)
    v_nbias = op("vector", lambda v: v.tensor_scalar_mul(
        bkq[:, NKV + NH:2 * (NKV + NH)], bkq[:, 0:NKV + NH], -1.0), inc=True)

    bx.wr("sync")
    vx = [dma("sync", x_tm[:, b * DM:(b + 1) * DM], x_in[b * 128:(b + 1) * 128, :])
          for b in range(NB)]
    bx.wdone(vx[-1])

    stQ = ExitStack()
    q8 = stQ.enter_context(nc.sbuf_tensor("q8", [128, 2 * 512], F8))

    _qst = {"st": [None, None]}

    def emit_quant_tiles(k0, k1):
        for k in range(k0, k1):
            p = k % 2
            wait("gpsimd", _qst["st"][p])
            v8 = dma("gpsimd", q8[:, p * 512:(p + 1) * 512],
                     AP(wsl_h.tensor, k * 128 * 512, [[512, 128], [1, 512]]))
            wait("sync", v8)
            _qst["st"][p] = dma("sync",
                                AP(ag_in, k * 128 * 512, [[512, 128], [1, 512]]),
                                q8[:, p * 512:(p + 1) * 512])

    def emit_ag():
        wait("gpsimd", _qst["st"][0])
        wait("gpsimd", _qst["st"][1])
        B.wait_all_dma("gpsimd", "sync")
        return B.op_cc("gpsimd", lambda g: g.collective_compute(
            "AllGather", ALU.bypass, replica_groups=[list(range(NCORES))],
            ins=[ag_in.ap()], outs=[ag_out.ap()]))

    def emit_layernorm(update_xa, v_stats_pre=None):
        jb = blnsb[0]
        junk = ln_sb[:, :]
        if v_stats_pre is not None:
            v_stats = v_stats_pre
        else:
            ev = None
            for b in range(NB):
                xb = x_tm[:, b * DM:(b + 1) * DM]
                bx.rd("scalar")
                if b == 0:
                    jb.wr("scalar")
                else:
                    B.waitf("scalar", ev)
                ev = op("scalar", lambda s, xb=xb, b=b: s.activation(
                    junk, xb, AF.Identity, accum_out=sca[:, C_LM + b:C_LM + b + 1]),
                    inc=True)
                B.waitf("scalar", ev)
                ev = op("scalar", lambda s, xb=xb, b=b: s.activation(
                    junk, xb, AF.Square, accum_out=sca[:, C_LI + b:C_LI + b + 1]),
                    inc=True)
            v_stats = ev
        wait("vector", v_stats)
        B.sop("vector", lambda v: v.tensor_scalar_mul(
            sca[:, C_LM:C_LM + 8], sca[:, C_LM:C_LM + 8], 1.0 / DM))
        B.sop("vector", lambda v: v.tensor_scalar_mul(
            sca[:, C_LI:C_LI + 8], sca[:, C_LI:C_LI + 8], 1.0 / DM))
        B.sop("vector", lambda v: v.tensor_mul(
            sca[:, C_LMI:C_LMI + 8], sca[:, C_LM:C_LM + 8], sca[:, C_LM:C_LM + 8]))
        vv = B.sop("vector", lambda v: v.tensor_sub(
            sca[:, C_LI:C_LI + 8], sca[:, C_LI:C_LI + 8], sca[:, C_LMI:C_LMI + 8]))
        wait("scalar", vv)
        wait("scalar", v_cst)
        B.waitf("scalar", v_stats)
        vstd = op("scalar", lambda s: s.activation(
            sca[:, C_L1I:C_L1I + 8], sca[:, C_LI:C_LI + 8], AF.Sqrt,
            bias=cst[:, 0:1]), inc=True)
        wait("vector", vstd)
        B.sop("vector", lambda v: v.reciprocal(sca[:, C_LI:C_LI + 8],
                                               sca[:, C_L1I:C_L1I + 8]))
        B.sop("vector", lambda v: v.tensor_mul(
            sca[:, C_LMI:C_LMI + 8], sca[:, C_LM:C_LM + 8], sca[:, C_LI:C_LI + 8]))
        B.sop("vector", lambda v: v.tensor_scalar_mul(
            sca[:, C_NMI:C_NMI + 8], sca[:, C_LMI:C_LMI + 8], -1.0))
        vmi = B.sop("vector", lambda v: v.tensor_scalar_add(
            sca[:, C_L1I:C_L1I + 8], sca[:, C_LI:C_LI + 8], 1.0))
        wait("scalar", vmi)
        for b in range(NB):
            xb = x_tm[:, b * DM:(b + 1) * DM]
            bx.rd("scalar")
            jb.wr("scalar")
            vln = op("scalar", lambda s, xb=xb, b=b: s.activation(
                junk, xb, AF.Identity, bias=sca[:, C_NMI + b:C_NMI + b + 1],
                scale=sca[:, C_LI + b:C_LI + b + 1]), inc=True)
            jb.wdone(vln)
            if update_xa:
                bx.wr("vector")
                wait("vector", vln)
                B.waitf("vector", vmi)
                vxa = op("vector", lambda v, xb=xb, b=b: v.tensor_scalar(
                    xb, xb, sca[:, C_L1I + b:C_L1I + b + 1],
                    sca[:, C_LMI + b:C_LMI + b + 1], ALU.mult, ALU.subtract),
                    inc=True)
                bx.wdone(vxa)
            jb.rd("tensor")
            wait("tensor", v_id)
            for half in range(2):
                tb = banks[4].ap().bitcast(BF16)
                bbuf[4].wr("tensor")
                for ii in range(8):
                    i = half * 8 + ii
                    op("tensor", lambda t, junk=junk, i=i, ii=ii, tb=tb: t.transpose(
                        tb[:, ii * 128:(ii + 1) * 128],
                        junk[:, i * 128:(i + 1) * 128], ident[:, :]))
                vt = mark("tensor")
                bbuf[4].wdone(vt)
                bln.wr("vector")
                bbuf[4].rd("vector")
                dst = AP(lnXt, (half * 8) * S + b * 128,
                         [[NI * S, 128], [S, 8], [1, 128]])
                vev = op("vector", lambda v, tb=tb, dst=dst: v.tensor_copy(
                    dst, tb[:, :].rearrange("p (c n) -> p c n", c=8)), inc=True)
                bbuf[4].rdone(vev)
                bln.wdone(vev)
            jb.rdone(mark("tensor"))

    for b in range(NB):
        wait("scalar", vx[b])
    emit_layernorm(update_xa=True)
    v_ln1 = ("p_v", B.sems["p_v"][1])

    stA = ExitStack()
    eA = stA.enter_context
    wk_g = eA(nc.sbuf_tensor("wk_g", [128, NI * 128], BF16))
    brow = eA(nc.sbuf_tensor("brow", [1, 512 + DM], BF16))
    wmbv_g = eA(nc.sbuf_tensor("wmbv_g", [128, NI * 640], BF16))
    mn_g = eA(nc.sbuf_tensor("mn_g", [128, 2 * NF * 258], BF16))
    xcat_k = eA(nc.sbuf_tensor("xcat_k", [128, 2 * S], BF16))
    xcat_tm = eA(nc.sbuf_tensor("xcat_tm", [128, NB * 259], BF16))
    dpk_fm = eA(nc.sbuf_tensor("dpk_fm", [128, NF * S], BF16))
    dpk_tm = eA(nc.sbuf_tensor("dpk_tm", [128, NB * F], BF16))
    mb_g = eA(nc.sbuf_tensor("mb_g", [128, NB * 512], BF16))
    v_g = eA(nc.sbuf_tensor("v_g", [128, NB * 128], BF16))
    pv_raw = eA(nc.sbuf_tensor("pv_raw", [128, NB * 516], BF16))
    wmv_h = eA(nc.sbuf_tensor("wmv_h", [128, NB * 128], BF16))
    dstage = eA(nc.sbuf_tensor("dstage", [128, F], BF16))

    dma("gpsimd", brow[0:1, 0:512], bv_h.unsqueeze(0))
    v_brow = dma("gpsimd", brow[0:1, 512:512 + DM], bmb_h.unsqueeze(0), cont=True)
    bwk = B.buf(wk_g, "wk_g")
    bwmbv = B.buf(wmbv_g, "wmbv_g")
    bmn = B.buf(mn_g, "mn_g")
    bxck = B.buf(xcat_k, "xcat_k")
    bxctm = B.buf(xcat_tm, "xcat_tm")
    bdpf = B.buf(dpk_fm, "dpk_fm")
    bdpt = B.buf(dpk_tm, "dpk_tm")
    bmb = B.buf(mb_g, "mb_g")
    bv_ = B.buf(v_g, "v_g")
    bvn = bv_
    bpv = B.buf(pv_raw, "pv_raw")
    bwmv = B.buf(wmv_h, "wmv_h")
    bds = B.buf(dstage, "dstage")
    bdn = bds
    bsca = B.buf(sca, "sca")

    dstage2 = eA(nc.sbuf_tensor("dstage2", [128, F], BF16))
    bds2 = B.buf(dstage2, "dstage2")

    for g in range(NKV):
        bwk.wr("gpsimd")
        v_wk = dma("gpsimd", wk_g[:, :],
                   AP(wkT.tensor, g * 128 * NI * 128,
                      [[NI * 128, 128], [1, NI * 128]]))
        bwk.wdone(v_wk)
        bwmbv.wr("gpsimd")
        v_wv = dma("gpsimd", wmbv_g[:, :],
                   AP(wmbv_h.tensor, g * 128 * NI * 640,
                      [[NI * 640, 128], [1, NI * 640]]))
        bwmbv.wdone(v_wv)
        bmn.wr("gpsimd")
        v_mn = dma("gpsimd", mn_g[:, :],
                   AP(mn_pre.tensor, g * 128 * 2 * NF * 258,
                      [[2 * NF * 258, 128], [1, 2 * NF * 258]]))
        bmn.wdone(v_mn)
        emit_quant_tiles(g * 18, (g + 1) * 18)

        bwk.rd("tensor")
        bln.rd("tensor")
        wait("tensor", v_ln1)
        bbuf[0].wr("tensor")
        bbuf[1].wr("tensor")
        for i in range(NI):
            for h2 in range(2):
                op("tensor", lambda t, i=i, h2=h2: t.matmul(
                    banks[h2][:, :], wk_g[:, i * 128:(i + 1) * 128],
                    lnXt[:, i * S + h2 * 512:i * S + (h2 + 1) * 512],
                    start=(i == 0), stop=(i == NI - 1)))
        v_kmm = mark("tensor")
        bwk.rdone(v_kmm)
        bbuf[0].wdone(v_kmm)
        bbuf[1].wdone(v_kmm)

        wait("scalar", v_nbias)
        bxck.wr("scalar")
        bbuf[0].rd("scalar")
        bbuf[1].rd("scalar")
        for h2 in range(2):
            op("scalar", lambda s, h2=h2, g=g: s.activation(
                xcat_k[:, h2 * 512:(h2 + 1) * 512], banks[h2][:, :], AF.Relu,
                bias=bkq[:, g:g + 1]))
            op("scalar", lambda s, h2=h2, g=g: s.activation(
                xcat_k[:, S + h2 * 512:S + (h2 + 1) * 512], banks[h2][:, :],
                AF.Relu, bias=bkq[:, NKV + NH + g:NKV + NH + g + 1], scale=-1.0))
        v_kevac = mark("scalar")
        bxck.wdone(v_kevac)
        bbuf[0].rdone(v_kevac)
        bbuf[1].rdone(v_kevac)

        wait("tensor", v_roll)
        bdpf.wr("vector")
        v_dpfm = None
        for j in (1, 2, 3):
            bxck.rd("tensor")
            for half in range(2):
                bks = (2, 3) if half == 0 else (0, 1)
                for c2 in range(2):
                    bbuf[bks[c2]].wr("tensor")
                    op("tensor", lambda t, j=j, half=half, c2=c2, bks=bks: t.matmul(
                        banks[bks[c2]][:, :],
                        rollm[:, (j - 1) * 128:j * 128],
                        xcat_k[:, half * S + c2 * 512:half * S + (c2 + 1) * 512],
                        start=True, stop=False))
                    op("tensor", lambda t, j=j, half=half, c2=c2, bks=bks: t.matmul(
                        banks[bks[c2]][:, :],
                        rollm[:, (2 + j) * 128:(3 + j) * 128],
                        xcat_k[:, (1 - half) * S + c2 * 512:
                               (1 - half) * S + (c2 + 1) * 512],
                        start=False, stop=True))
            v_shmm = mark("tensor")
            for bk in (0, 1, 2, 3):
                bbuf[bk].wdone(v_shmm)
            bxck.rdone(v_shmm)
            wait("vector", v_shmm)
            bxck.rd("vector")
            for half in range(2):
                bks = (2, 3) if half == 0 else (0, 1)
                t6 = (j - 1) * 2 + half
                for c2 in range(2):
                    bbuf[bks[c2]].rd("vector")
                    v_dpfm = op("vector", lambda v, t6=t6, half=half, c2=c2,
                                bks=bks: v.tensor_mul(
                        dpk_fm[:, t6 * S + c2 * 512:t6 * S + (c2 + 1) * 512],
                        xcat_k[:, half * S + c2 * 512:half * S + (c2 + 1) * 512],
                        banks[bks[c2]][:, :]), inc=True)
            for bk in (0, 1, 2, 3):
                bbuf[bk].rdone(v_dpfm)
            bxck.rdone(v_dpfm)
        bdpf.wdone(v_dpfm)

        bwmbv.rd("tensor")
        bln.rd("tensor")
        wait("tensor", v_brow)
        wait("tensor", v_ones)
        for b in range(NB):
            bbuf[2].wr("tensor")
            bbuf[3].wr("tensor")
            for i in range(NI):
                op("tensor", lambda t, i=i, b=b: t.matmul(
                    banks[2][:, :], lnXt[:, i * S + b * 128:i * S + (b + 1) * 128],
                    wmbv_g[:, i * 640:i * 640 + 512], start=(i == 0), stop=False))
                op("tensor", lambda t, i=i, b=b: t.matmul(
                    banks[3][:, 0:128], lnXt[:, i * S + b * 128:i * S + (b + 1) * 128],
                    wmbv_g[:, i * 640 + 512:(i + 1) * 640], start=(i == 0), stop=False))
            op("tensor", lambda t, g=g: t.matmul(
                banks[2][:, :], ones_row[0:1, :],
                brow[0:1, 512 + g * 512:512 + (g + 1) * 512], start=False, stop=True))
            v_mbvmm = op("tensor", lambda t, g=g: t.matmul(
                banks[3][:, 0:128], ones_row[0:1, :],
                brow[0:1, g * 128:(g + 1) * 128], start=False, stop=True), inc=True)
            bbuf[2].wdone(v_mbvmm)
            bbuf[3].wdone(v_mbvmm)
            bmb.wr("scalar")
            bv_.wr("scalar")
            bbuf[2].rd("scalar")
            bbuf[3].rd("scalar")
            op("scalar", lambda s, b=b: s.activation(
                mb_g[:, b * 512:(b + 1) * 512], banks[2][:, :], AF.Sigmoid))
            v_vev = op("scalar", lambda s, b=b: s.activation(
                v_g[:, b * 128:(b + 1) * 128], banks[3][:, 0:128], AF.Identity),
                inc=True)
            bbuf[2].rdone(v_vev)
            bbuf[3].rdone(v_vev)
            if b == NB - 1:
                bmb.wdone(v_vev)
                bv_.wdone(v_vev)
        bwmbv.rdone(("p_t", B.sems["p_t"][1]))

        tbk = banks[4].ap().bitcast(BF16)
        bxck.rd("tensor")
        bxctm.wr("vector")
        for half in range(2):
            bbuf[4].wr("tensor")
            for b in range(NB):
                op("tensor", lambda t, b=b, half=half, tbk=tbk: t.transpose(
                    tbk[:, b * 128:(b + 1) * 128],
                    xcat_k[:, half * S + b * 128:half * S + (b + 1) * 128],
                    ident[:, :]))
            v_ktr = mark("tensor")
            bbuf[4].wdone(v_ktr)
            bbuf[4].rd("vector")
            v_xc1 = op("vector", lambda v, half=half, tbk=tbk: v.tensor_copy(
                AP(xcat_tm, 3 + half * 128, [[NB * 259, 128], [259, NB], [1, 128]]),
                tbk[:, 0:NB * 128].rearrange("p (b n) -> p b n", b=NB)), inc=True)
            bbuf[4].rdone(v_xc1)
        bxck.rdone(v_xc1)
        bxctm.wdone(v_xc1)
        bxctm.wr("vector")
        vwrap = op("vector", lambda v: v.tensor_copy(
            AP(xcat_tm, 0, [[NB * 259, 128], [259, NB], [1, 3]]),
            AP(xcat_tm, 256, [[NB * 259, 128], [259, NB], [1, 3]])), inc=True)
        B.waitf("vector", vwrap)
        bdpt.wr("vector")
        for b in range(NB):
            op("vector", lambda v, b=b: v.tensor_tensor(
                AP(dpk_tm, b * F, [[NB * F, 128], [256, 3], [1, 256]]),
                AP(xcat_tm, b * 259 + 3, [[NB * 259, 128], [0, 3], [1, 256]]),
                AP(xcat_tm, b * 259 + 2, [[NB * 259, 128], [-1, 3], [1, 256]]),
                ALU.mult))
        v_dptm = mark("vector")
        bxctm.wdone(v_dptm)
        bdpt.wdone(v_dptm)

        bdpt.rd("scalar")
        blnsb[0].wr("scalar")
        bsca.wr("scalar")
        for b in range(NB):
            ev = op("scalar", lambda s, b=b: s.activation(
                ln_sb[:, 0:F], dpk_tm[:, b * F:(b + 1) * F], AF.Square,
                accum_out=sca[:, C_SSQ + b:C_SSQ + b + 1]), inc=True)
            B.waitf("scalar", ev)
        v_ssq = ev
        blnsb[0].wdone(v_ssq)
        bdpt.rdone(v_ssq)

        wait("vector", v_ssq)
        B.waitf("scalar", v_ssq)
        op("scalar", lambda s: s.activation(sca[:, C_N:C_N + 8],
                                            sca[:, C_SSQ:C_SSQ + 8], AF.Sqrt))
        v_sc1 = op("scalar", lambda s: s.activation(
            sca[:, C_EN:C_EN + 8], sca[:, C_SSQ:C_SSQ + 8], AF.Sqrt,
            scale=cst[:, 1:2]), inc=True)
        wait("vector", v_sc1)
        B.sop("vector", lambda v: v.tensor_scalar_max(sca[:, C_N:C_N + 8],
                                                      sca[:, C_N:C_N + 8], 1e-12))
        B.sop("vector", lambda v: v.tensor_scalar_max(sca[:, C_EN:C_EN + 8],
                                                      sca[:, C_EN:C_EN + 8], 1e-20))
        B.sop("vector", lambda v: v.reciprocal(sca[:, C_INVN:C_INVN + 8],
                                               sca[:, C_N:C_N + 8]))
        B.sop("vector", lambda v: v.reciprocal(sca[:, C_RSSQ:C_RSSQ + 8],
                                               sca[:, C_SSQ:C_SSQ + 8]))
        bvn.wr("vector")
        for b in range(NB):
            B.sop("vector", lambda v, b=b: v.tensor_scalar_mul(
                v_g[:, b * 128:(b + 1) * 128], v_g[:, b * 128:(b + 1) * 128],
                sca[:, C_INVN + b:C_INVN + b + 1]))
        v_vn = mark("vector")
        bvn.wdone(v_vn)

        bdpf.rd("tensor")
        bmn.rd("tensor")
        bpv.wr("vector")
        for b in range(NB):
            bbuf[(b % 2) * 2].wr("tensor")
            bbuf[(b % 2) * 2 + 1].wr("tensor")
            for c in range(NF):
                for pair in range(2):
                    op("tensor", lambda t, b=b, c=c, pair=pair, nb0=(b % 2) * 2:
                       t.matmul(
                        banks[nb0 + pair][:, 0:258],
                        dpk_fm[:, c * S + b * 128:c * S + (b + 1) * 128],
                        mn_g[:, (pair * NF + c) * 258:(pair * NF + c + 1) * 258],
                        start=(c == 0), stop=(c == NF - 1)))
            nb0 = (b % 2) * 2
            v_nmm = mark("tensor")
            bbuf[nb0].wdone(v_nmm)
            bbuf[nb0 + 1].wdone(v_nmm)
            bbuf[nb0].rd("vector")
            bbuf[nb0 + 1].rd("vector")
            op("vector", lambda v, b=b, nb0=nb0: v.tensor_copy(
                pv_raw[:, b * 516:b * 516 + 258], banks[nb0][:, 0:258]))
            v_pvev = op("vector", lambda v, b=b, nb0=nb0: v.tensor_copy(
                pv_raw[:, b * 516 + 258:(b + 1) * 516], banks[nb0 + 1][:, 0:258]),
                inc=True)
            bbuf[nb0].rdone(v_pvev)
            bbuf[nb0 + 1].rdone(v_pvev)
        bpv.wdone(v_pvev)
        v_numt_last = ("p_t", B.sems["p_t"][1])
        bdpf.rdone(v_numt_last)
        bmn.rdone(v_numt_last)

        bpv.rd("vector")
        B.sop("vector", lambda v: v.tensor_copy(
            sca[:, C_RK:C_RK + 32],
            AP(pv_raw, 128, [[NB * 516, 128], [516, NB], [129, 4]])))
        B.sop("vector", lambda v: v.tensor_tensor(
            sca[:, C_DPV:C_DPV + 32], sca[:, C_RK:C_RK + 32],
            AP(sca, C_EN, [[288, 128], [1, 8], [0, 4]]), ALU.add))
        B.sop("vector", lambda v: v.tensor_tensor(
            sca[:, C_TMUL:C_TMUL + 32],
            AP(sca, C_N, [[288, 128], [1, 8], [0, 4]]),
            sca[:, C_DPV:C_DPV + 32], ALU.mult))
        B.sop("vector", lambda v: v.reciprocal(sca[:, C_NC1:C_NC1 + 32],
                                               sca[:, C_TMUL:C_TMUL + 32]))
        B.sop("vector", lambda v: v.tensor_scalar_mul(sca[:, C_NC1:C_NC1 + 32],
                                                      sca[:, C_NC1:C_NC1 + 32], -1.0))
        v_ch1 = B.sop("vector", lambda v: v.tensor_tensor(
            sca[:, C_COEF:C_COEF + 32], sca[:, C_TMUL:C_TMUL + 32],
            AP(sca, C_RSSQ, [[288, 128], [1, 8], [0, 4]]), ALU.mult))
        wait("scalar", v_ch1)
        B.waitf("scalar", v_sc1)
        v_ch2 = op("scalar", lambda s: s.activation(
            sca[:, C_COEF:C_COEF + 32], sca[:, C_COEF:C_COEF + 32], AF.Relu,
            scale=-1.0, bias=1.0), inc=True)
        wait("vector", v_ch2)
        B.sop("vector", lambda v: v.tensor_scalar_min(sca[:, C_COEF:C_COEF + 32],
                                                      sca[:, C_COEF:C_COEF + 32], 1.0))
        bcoef.wr("vector")
        v_chain = B.sop("vector", lambda v: v.tensor_tensor(
            coefb[:, :], sca[:, C_COEF:C_COEF + 32],
            AP(sca, C_INVN, [[288, 128], [1, 8], [0, 4]]), ALU.mult))
        bcoef.wdone(v_chain)
        bsca.wdone(v_chain)

        for j in range(4):
            h = 4 * g + j
            bwmv.wr("vector")
            bpv.rd("vector")
            bvn.rd("vector")
            bmb.rd("vector")
            for b in range(NB):
                B.sop("vector", lambda v, b=b, j=j: v.scalar_tensor_tensor(
                    wmv_h[:, b * 128:(b + 1) * 128],
                    pv_raw[:, b * 516 + j * 129:b * 516 + j * 129 + 128],
                    sca[:, C_NC1 + 4 * b + j:C_NC1 + 4 * b + j + 1],
                    v_g[:, b * 128:(b + 1) * 128], ALU.mult, ALU.add))
                B.sop("vector", lambda v, b=b, j=j: v.tensor_mul(
                    wmv_h[:, b * 128:(b + 1) * 128],
                    wmv_h[:, b * 128:(b + 1) * 128],
                    mb_g[:, b * 512 + j * 128:b * 512 + (j + 1) * 128]))
            v_wmv = mark("vector")
            bwmv.wdone(v_wmv)
            bpv.rdone(v_wmv)
            bvn.rdone(v_wmv)
            bmb.rdone(v_wmv)
            bwmv.rd("tensor")
            bdpt.rd("tensor")
            bbuf[6].wr("tensor")
            bbuf[7].wr("tensor")
            for b in range(NB):
                op("tensor", lambda t, b=b: t.matmul(
                    banks[6][:, :], wmv_h[:, b * 128:(b + 1) * 128],
                    dpk_tm[:, b * F:b * F + 512], start=(b == 0), stop=(b == NB - 1)))
                op("tensor", lambda t, b=b: t.matmul(
                    banks[7][:, 0:256], wmv_h[:, b * 128:(b + 1) * 128],
                    dpk_tm[:, b * F + 512:(b + 1) * F], start=(b == 0),
                    stop=(b == NB - 1)))
            v_dmm = mark("tensor")
            bwmv.rdone(v_dmm)
            bdpt.rdone(v_dmm)
            bbuf[6].wdone(v_dmm)
            bbuf[7].wdone(v_dmm)
            bds.wr("scalar")
            bbuf[6].rd("scalar")
            bbuf[7].rd("scalar")
            op("scalar", lambda s: s.activation(dstage[:, 0:512], banks[6][:, :],
                                                AF.Identity), inc=True)
            v_dev = op("scalar", lambda s: s.activation(
                dstage[:, 512:F], banks[7][:, 0:256], AF.Identity), inc=True)
            bbuf[6].rdone(v_dev)
            bbuf[7].rdone(v_dev)
            bds.wdone(v_dev)
            bds.rd("tensor")
            wait("tensor", v_id)
            bbuf[6].wr("tensor")
            bbuf[7].wr("tensor")
            for ct in range(6):
                bk_, cc = (banks[6], ct) if ct < 4 else (banks[7], ct - 4)
                op("tensor", lambda t, ct=ct, bk_=bk_, cc=cc: t.transpose(
                    bk_.bitcast(BF16)[:, cc * 128:(cc + 1) * 128],
                    dstage[:, ct * 128:(ct + 1) * 128], ident[:, :]))
            v_dtr = mark("tensor")
            bds.rdone(v_dtr)
            bbuf[6].wdone(v_dtr)
            bbuf[7].wdone(v_dtr)
            bds2.wr("scalar")
            bbuf[6].rd("scalar")
            bbuf[7].rd("scalar")
            op("scalar", lambda s: s.activation(
                dstage2[:, 0:512], banks[6].bitcast(BF16)[:, 0:512],
                AF.Identity), inc=True)
            v_dev2 = op("scalar", lambda s: s.activation(
                dstage2[:, 512:F], banks[7].bitcast(BF16)[:, 0:256],
                AF.Identity), inc=True)
            bbuf[6].rdone(v_dev2)
            bbuf[7].rdone(v_dev2)
            bds2.wdone(v_dev2)
            bds2.rd("sync")
            v_dd = dma("sync",
                       AP(ard_in, h * 128,
                          [[NH * 128, 128], [128 * NH * 128, 6], [1, 128]]),
                       dstage2[:, :].rearrange("p (c n) -> p c n", c=6))
            bds2.rdone(v_dd)

        bcoef.rd("tensor")
        bdpt.rd("tensor")
        bbuf[5].wr("tensor")
        bbuf[4].wr("tensor")
        for b in range(NB):
            op("tensor", lambda t, b=b: t.matmul(
                banks[5][0:4, :], coefb[:, 4 * b:4 * (b + 1)],
                dpk_tm[:, b * F:b * F + 512], start=(b == 0), stop=(b == NB - 1)))
            op("tensor", lambda t, b=b: t.matmul(
                banks[4][0:4, 0:256], coefb[:, 4 * b:4 * (b + 1)],
                dpk_tm[:, b * F + 512:(b + 1) * F], start=(b == 0),
                stop=(b == NB - 1)))
        v_dnmm = mark("tensor")
        bcoef.rdone(v_dnmm)
        bdpt.rdone(v_dnmm)
        bbuf[5].wdone(v_dnmm)
        bbuf[4].wdone(v_dnmm)
        bdn.wr("scalar")
        bbuf[5].rd("scalar")
        bbuf[4].rd("scalar")
        op("scalar", lambda s: s.activation(dstage[0:4, 0:512], banks[5][0:4, :],
                                            AF.Identity), inc=True)
        v_dnev = op("scalar", lambda s: s.activation(
            dstage[0:4, 512:F], banks[4][0:4, 0:256], AF.Identity), inc=True)
        bbuf[5].rdone(v_dnev)
        bbuf[4].rdone(v_dnev)
        bdn.wdone(v_dnev)
        bdn.rd("sync")
        v_dnd = dma("sync", AP(arn_in, 4 * g * F, [[F, 4], [1, F]]),
                    dstage[0:4, :])
        bdn.rdone(v_dnd)

    B.wait_all_dma("gpsimd", "sync")
    _arh = 384 * NH * 128
    v_ar1 = B.op_cc("gpsimd", lambda g2: g2.collective_compute(
        "AllReduce", ALU.add, replica_groups=[list(range(NCORES))],
        ins=[AP(ard_in, 0, [[1, _arh]])], outs=[AP(ard_out, 0, [[1, _arh]])]))
    v_arn = B.op_cc("gpsimd", lambda g2: g2.collective_compute(
        "AllReduce", ALU.add, replica_groups=[list(range(NCORES))],
        ins=[arn_in.ap()], outs=[arn_out.ap()]))
    v_ar = B.op_cc("gpsimd", lambda g2: g2.collective_compute(
        "AllReduce", ALU.add, replica_groups=[list(range(NCORES))],
        ins=[AP(ard_in, _arh, [[1, _arh]])],
        outs=[AP(ard_out, _arh, [[1, _arh]])]))

    B.join()
    stA.close()
    stQ.close()

    stB = ExitStack()
    eB = stB.enter_context
    newmm = eB(nc.sbuf_tensor("newmm", [128, NF * NH * 129], BF16))
    memh = eB(nc.sbuf_tensor("memh", [128, NB * DM], BF16))
    bnew = B.buf(newmm, "newmm")
    bmemh = B.buf(memh, "memh")

    bnew.wr("gpsimd")
    v_mnld = dma("gpsimd", newmm[:, :],
                 AP(mn_seq.tensor, 0, [[NF * 2064, 128], [1, NF * 2064]]))
    bnew.wdone(v_mnld)

    stB2 = ExitStack()
    eB2 = stB2.enter_context
    wq_h = eB2(nc.sbuf_tensor("wq_h", [128, NI * 128], BF16))
    xcat_q = eB2(nc.sbuf_tensor("xcat_q", [128, 2 * S], BF16))
    dpq = eB2(nc.sbuf_tensor("dpq", [128, 2 * NF * S], BF16))
    sqslots = eB2(nc.sbuf_tensor("sqslots", [128, 2 * S], BF16))
    rowbuf = ln_sb[0:1, :].bitcast(F32)
    bwq_ = B.buf(wq_h, "wq_h")
    bwq = [bwq_, bwq_]
    bxcq_ = B.buf(xcat_q, "xcat_q")
    bxcq = [bxcq_, bxcq_]
    bdpq = [B.buf(dpq, "dpq0"), B.buf(dpq, "dpq1")]
    bsqs = [B.buf(sqslots, "sq_s0"), B.buf(sqslots, "sq_s1")]
    brown = blnsb[0]

    pend = {}

    def q_prep(h):
        par = h % 2
        xq = 0
        sq = 0
        dq = par * NF * S
        bwq[par].wr("gpsimd")
        v_wq = dma("gpsimd", wq_h[:, :],
                   AP(wqT.tensor, h * 128 * NI * 128,
                      [[NI * 128, 128], [1, NI * 128]]))
        bwq[par].wdone(v_wq)
        bwq[par].rd("tensor")
        bln.rd("tensor")
        bbuf[0].wr("tensor")
        bbuf[1].wr("tensor")
        for i in range(NI):
            for h2 in range(2):
                op("tensor", lambda t, i=i, h2=h2: t.matmul(
                    banks[h2][:, :],
                    wq_h[:, i * 128:(i + 1) * 128],
                    lnXt[:, i * S + h2 * 512:i * S + (h2 + 1) * 512],
                    start=(i == 0), stop=(i == NI - 1)))
        v_qmm = mark("tensor")
        bwq[par].rdone(v_qmm)
        bbuf[0].wdone(v_qmm)
        bbuf[1].wdone(v_qmm)
        bxcq[par].wr("scalar")
        bbuf[0].rd("scalar")
        bbuf[1].rd("scalar")
        for h2 in range(2):
            op("scalar", lambda s, h2=h2, h=h, xq=xq: s.activation(
                xcat_q[:, xq + h2 * 512:xq + (h2 + 1) * 512], banks[h2][:, :],
                AF.Relu, bias=bkq[:, NKV + h:NKV + h + 1]))
            op("scalar", lambda s, h2=h2, h=h, xq=xq: s.activation(
                xcat_q[:, xq + S + h2 * 512:xq + S + (h2 + 1) * 512],
                banks[h2][:, :], AF.Relu,
                bias=bkq[:, 2 * NKV + NH + h:2 * NKV + NH + h + 1], scale=-1.0))
        v_qevac = mark("scalar")
        bxcq[par].wdone(v_qevac)
        bbuf[0].rdone(v_qevac)
        bbuf[1].rdone(v_qevac)
        bdpq[par].wr("vector")
        wait("tensor", v_roll)
        v_dpq = None
        for j in (1, 2, 3):
            bxcq[par].rd("tensor")
            for half in range(2):
                bks = (2, 3) if half == 0 else (0, 1)
                for c2 in range(2):
                    bbuf[bks[c2]].wr("tensor")
                    op("tensor", lambda t, j=j, half=half, c2=c2, bks=bks: t.matmul(
                        banks[bks[c2]][:, :],
                        rollm[:, (j - 1) * 128:j * 128],
                        xcat_q[:, half * S + c2 * 512:half * S + (c2 + 1) * 512],
                        start=True, stop=False))
                    op("tensor", lambda t, j=j, half=half, c2=c2, bks=bks: t.matmul(
                        banks[bks[c2]][:, :],
                        rollm[:, (2 + j) * 128:(3 + j) * 128],
                        xcat_q[:, (1 - half) * S + c2 * 512:
                               (1 - half) * S + (c2 + 1) * 512],
                        start=False, stop=True))
            v_shmm = mark("tensor")
            for bk in (0, 1, 2, 3):
                bbuf[bk].wdone(v_shmm)
            bxcq[par].rdone(v_shmm)
            wait("vector", v_shmm)
            bxcq[par].rd("vector")
            for half in range(2):
                bks = (2, 3) if half == 0 else (0, 1)
                t6 = (j - 1) * 2 + half
                for c2 in range(2):
                    bbuf[bks[c2]].rd("vector")
                    v_dpq = op("vector", lambda v, t6=t6, half=half, dq=dq,
                               c2=c2, bks=bks: v.tensor_mul(
                        dpq[:, dq + t6 * S + c2 * 512:dq + t6 * S + (c2 + 1) * 512],
                        xcat_q[:, half * S + c2 * 512:half * S + (c2 + 1) * 512],
                        banks[bks[c2]][:, :]), inc=True)
            for bk in (0, 1, 2, 3):
                bbuf[bk].rdone(v_dpq)
            bxcq[par].rdone(v_dpq)
        bdpq[par].wdone(v_dpq)
        pend[("dpq", h)] = v_dpq

    def q_prep_b(h):
        par = h % 2
        dq = par * NF * S
        wait("scalar", pend[("dpq", h)])
        wait("tensor", v_ones)
        bbuf[5].wr("tensor")
        bbuf[6].wr("tensor")
        v_mm = None
        for c in range(NF):
            sl = c % 2
            slot = sqslots[:, sl * S:(sl + 1) * S]
            bsqs[sl].wr("scalar")
            ev = op("scalar", lambda s, c=c, dq=dq, slot=slot: s.activation(
                slot, dpq[:, dq + c * S:dq + (c + 1) * S], AF.Square), inc=True)
            bsqs[sl].wdone(ev)
            wait("tensor", ev)
            bsqs[sl].rd("tensor")
            for h2 in range(2):
                op("tensor", lambda t, c=c, h2=h2, slot=slot: t.matmul(
                    banks[5 + h2][0:1, :], ones_col[:, 0:1],
                    slot[:, h2 * 512:(h2 + 1) * 512],
                    start=(c == 0), stop=(c == NF - 1)))
            v_mm = mark("tensor")
            bsqs[sl].rdone(v_mm)
        bbuf[5].wdone(v_mm)
        bbuf[6].wdone(v_mm)
        pend[("rows", h)] = v_mm

    def q_fin(h):
        par = h % 2
        brown.wr("scalar")
        bbuf[5].rd("scalar")
        bbuf[6].rd("scalar")
        wait("scalar", pend[("rows", h)])
        op("scalar", lambda s: s.activation(rowbuf[0:1, 0:512], banks[5][0:1, :],
                                            AF.Identity), inc=True)
        v_row = op("scalar", lambda s: s.activation(
            rowbuf[0:1, 512:1024], banks[6][0:1, :], AF.Identity), inc=True)
        bbuf[5].rdone(v_row)
        bbuf[6].rdone(v_row)
        brown.wdone(v_row)
        brown.rd("tensor")
        wait("tensor", v_row)
        bbuf[7].wr("tensor")
        for b in range(NB):
            op("tensor", lambda t, b=b: t.matmul(
                banks[7][:, b:b + 1], rowbuf[0:1, b * 128:(b + 1) * 128],
                ones_f32[0:1, 0:1], start=True, stop=True))
        v_mini = mark("tensor")
        brown.rdone(v_mini)
        bbuf[7].wdone(v_mini)
        wait("scalar", v_mini)
        B.waitf("scalar", pend.get(("qen_s", par)))
        wait("scalar", pend.get(("qen_rd", par)))
        bbuf[7].rd("scalar")
        v_qssq = op("scalar", lambda s, par=par: s.activation(
            sca[:, C_QSSQ + par * 8:C_QSSQ + par * 8 + 8], banks[7][:, 0:8],
            AF.Identity), inc=True)
        bbuf[7].rdone(v_qssq)
        B.waitf("scalar", v_qssq)
        v_qen = op("scalar", lambda s, par=par: s.activation(
            sca[:, C_QEN + par * 8:C_QEN + par * 8 + 8],
            sca[:, C_QSSQ + par * 8:C_QSSQ + par * 8 + 8], AF.Sqrt,
            scale=cst[:, 1:2]), inc=True)
        pend[("qen_s", par)] = v_qen
        wait("vector", v_qen)
        v_qen2 = B.sop("vector", lambda v, par=par: v.tensor_scalar_max(
            sca[:, C_QEN + par * 8:C_QEN + par * 8 + 8],
            sca[:, C_QEN + par * 8:C_QEN + par * 8 + 8], 1e-20))
        pend[("qen", h)] = v_qen2

    def num_a(h):
        par = h % 2
        dq = par * NF * S
        bnew.rd("tensor")
        bdpq[par].rd("tensor")
        groups = [(2, [0, 1, 2]), (3, [3, 4, 5]), (4, [6, 7])]
        evs = []
        for bank_i, bs in groups:
            bk_ = banks[bank_i]
            bbuf[bank_i].wr("tensor")
            for k, b in enumerate(bs):
                for c in range(NF):
                    op("tensor", lambda t, b=b, k=k, c=c, h=h, bk_=bk_, dq=dq: t.matmul(
                        bk_[:, k * 129:(k + 1) * 129],
                        dpq[:, dq + c * S + b * 128:dq + c * S + (b + 1) * 128],
                        newmm[:, c * NH * 129 + h * 129:c * NH * 129 + (h + 1) * 129],
                        start=(c == 0), stop=(c == NF - 1)))
            v_na = mark("tensor")
            bbuf[bank_i].wdone(v_na)
        cw = C_NAW + par * 8
        ce = C_QEN + par * 8
        wait("scalar", pend[("qen", h)])
        B.waitf("scalar", pend.get(("naw_s", par)))
        ev_w = None
        for gi_, (bank_i, bs) in enumerate(groups):
            bbuf[bank_i].rd("scalar")
            ev_w = op("scalar", lambda s, bank_i=bank_i, bs=bs, cw=cw: s.activation(
                sca[:, cw + bs[0]:cw + bs[0] + len(bs)],
                AP(banks[bank_i], 128, [[512, 128], [129, len(bs)]]),
                AF.Identity), inc=True)
        wait("vector", ev_w)
        B.sop("vector", lambda v, cw=cw, ce=ce: v.tensor_add(
            sca[:, cw:cw + 8], sca[:, cw:cw + 8], sca[:, ce:ce + 8]))
        v_inv = B.sop("vector", lambda v, cw=cw: v.reciprocal(sca[:, cw:cw + 8],
                                                              sca[:, cw:cw + 8]))
        wait("scalar", v_inv)
        v_mh = None
        for bank_i, bs in groups:
            for k, b in enumerate(bs):
                v_mh = op("scalar", lambda s, b=b, k=k, h=h, bank_i=bank_i, cw=cw:
                          s.activation(
                              memh[:, b * DM + h * 128:b * DM + (h + 1) * 128],
                              banks[bank_i][:, k * 129:k * 129 + 128],
                              AF.Identity, scale=sca[:, cw + b:cw + b + 1]),
                          inc=True)
            bbuf[bank_i].rdone(v_mh)
        bdpq[par].rdone(("p_t", B.sems["p_t"][1]))
        pend[("qen_rd", par)] = v_mh
        pend[("naw_s", par)] = v_mh
        if h == 0:
            bmemh.wr("vector")
        if h == NH - 1:
            bmemh.wdone(v_mh)

    q_prep(0)
    q_prep_b(0)
    q_fin(0)
    q_prep(1)
    q_prep_b(1)
    q_fin(1)

    stB1 = ExitStack()
    dadd = stB1.enter_context(nc.sbuf_tensor("dadd", [128, 2 * 1024], BF16))
    normrows = stB1.enter_context(nc.sbuf_tensor("normrows", [16, F], BF16))
    bdadd = [B.buf(dadd, "dadd0"), B.buf(dadd, "dadd1")]
    bnr = B.buf(normrows, "normrows")
    wait("sync", v_arn)
    bnr.wr("sync")
    v_nr = dma("sync", normrows[0:16, :], AP(arn_out, 0, [[F, 16], [1, F]]))
    bnr.wdone(v_nr)
    bnew.wr("vector")
    for c in range(NF):
        for hf in range(2):
            k = 2 * c + hf
            sl = k % 2
            qn = ("sync", "scalar")[k % 2]
            wait(qn, v_ar1 if c < 3 else v_ar)
            bdadd[sl].wr(qn)
            v_d1 = dma(qn, dadd[:, sl * 1024:(sl + 1) * 1024],
                       AP(ard_out, c * 128 * NH * 128 + hf * 1024,
                          [[NH * 128, 128], [1, 1024]]))
            bdadd[sl].wdone(v_d1)
            wait("vector", v_d1)
            bdadd[sl].rd("vector")
            v_na = op("vector", lambda v, c=c, hf=hf, sl=sl: v.tensor_tensor(
                AP(newmm, c * NH * 129 + hf * 8 * 129,
                   [[NF * NH * 129, 128], [129, 8], [1, 128]]),
                AP(newmm, c * NH * 129 + hf * 8 * 129,
                   [[NF * NH * 129, 128], [129, 8], [1, 128]]),
                AP(dadd, sl * 1024, [[2 * 1024, 128], [128, 8], [1, 128]]),
                ALU.add), inc=True)
            bdadd[sl].rdone(v_na)
        bnr.rd("tensor")
        wait("tensor", v_id)
        bbuf[2].wr("tensor")
        wait("tensor", v_nr)
        op("tensor", lambda t, c=c: t.transpose(
            banks[2].bitcast(BF16)[:, 0:NH], normrows[0:16, c * 128:(c + 1) * 128],
            ident[0:16, 0:16]))
        v_tr = mark("tensor")
        bnr.rdone(v_tr)
        bbuf[2].wdone(v_tr)
        bbuf[2].rd("vector")
        v_nadd = op("vector", lambda v, c=c: v.tensor_add(
            AP(newmm, c * NH * 129 + 128, [[NF * NH * 129, 128], [129, NH]]),
            AP(newmm, c * NH * 129 + 128, [[NF * NH * 129, 128], [129, NH]]),
            banks[2].bitcast(BF16)[:, 0:NH]), inc=True)
        bbuf[2].rdone(v_nadd)
    bnew.wdone(v_nadd)
    wait("gpsimd", v_nadd)
    v_ag = emit_ag()
    stB1.close()

    for h in range(NH):
        num_a(h)
        if h + 2 < NH:
            q_prep(h + 2)
        if h + 1 < NH and h >= 1:
            q_prep_b(h + 1)
            q_fin(h + 1)

    if DEBUG_TAPS:
        bmemh.rd("sync")
        dma("sync", dbg_memh, memh[:, :])
        bnew.rd("sync")
        dma("sync", dbg_nm, newmm[:, :])

    fenceB2 = B.fence_evs()
    stB2.close()
    stB3 = ExitStack()
    eB3 = stB3.enter_context
    wg_f = eB3(nc.sbuf_tensor("wg_f", [128, NI * 512], BF16))
    sgf32 = eB3(nc.sbuf_tensor("sgf32", [128, 512], F32))
    tscr = eB3(nc.sbuf_tensor("tscr", [128, 512], F32))
    bgrow = eB3(nc.sbuf_tensor("bgrow", [1, DM], BF16))
    bwg = B.buf(wg_f, "wg_f")
    bsg = B.buf(sgf32, "sgf32")
    bts = B.buf(tscr, "tscr")
    bbg = B.buf(bgrow, "bgrow")
    for _b in (bwg, bsg, bts, bbg):
        _b.seed(fenceB2)
    bbg.wr("gpsimd")
    v_bgrow = dma("gpsimd", bgrow[0:1, :], bg_h.unsqueeze(0))
    bbg.wdone(v_bgrow)
    _ln2st = [None]
    for og in range(4):
        bwg.wr("gpsimd")
        v_wg = dma("gpsimd", wg_f[:, :],
                   AP(wgT.tensor, og * 128 * NI * 512,
                      [[NI * 512, 128], [1, NI * 512]]))
        bwg.wdone(v_wg)
        bwg.rd("tensor")
        bln.rd("tensor")
        wait("tensor", v_bgrow)
        for b in range(NB):
            bbuf[0].wr("tensor")
            for i in range(NI):
                op("tensor", lambda t, i=i, b=b: t.matmul(
                    banks[0][:, :], lnXt[:, i * S + b * 128:i * S + (b + 1) * 128],
                    wg_f[:, i * 512:(i + 1) * 512], start=(i == 0), stop=False))
            v_gmm = op("tensor", lambda t, og=og: t.matmul(
                banks[0][:, :], ones_row[0:1, :],
                bgrow[0:1, og * 512:(og + 1) * 512], start=False, stop=True),
                inc=True)
            bbuf[0].wdone(v_gmm)
            bsg.wr("scalar")
            bbuf[0].rd("scalar")
            v_sg = op("scalar", lambda s: s.activation(
                sgf32[:, :], banks[0][:, :], AF.Sigmoid), inc=True)
            bbuf[0].rdone(v_sg)
            bsg.wdone(v_sg)
            bsg.rd("vector")
            bmemh.rd("vector")
            bts.wr("vector")
            B.sop("vector", lambda v, b=b, og=og: v.tensor_mul(
                tscr[:, :], sgf32[:, :],
                memh[:, b * DM + og * 512:b * DM + (og + 1) * 512]))
            bx.wr("vector")
            v_ao = B.sop("vector", lambda v, b=b, og=og: v.tensor_add(
                x_tm[:, b * DM + og * 512:b * DM + (og + 1) * 512],
                x_tm[:, b * DM + og * 512:b * DM + (og + 1) * 512],
                tscr[:, :]))
            bsg.rdone(v_ao)
            bmemh.rdone(v_ao)
            bts.wdone(v_ao)
            bx.wdone(v_ao)
            if False:
                jb3 = blnsb[0]
                junk3 = ln_sb[:, :]
                bx.rd("scalar")
                wait("scalar", v_ao)
                if b == 0:
                    jb3.wr("scalar")
                else:
                    B.waitf("scalar", _ln2st[0])
                _ln2st[0] = op("scalar", lambda s, b=b: s.activation(
                    junk3, x_tm[:, b * DM:(b + 1) * DM], AF.Identity,
                    accum_out=sca[:, C_LM + b:C_LM + b + 1]), inc=True)
                B.waitf("scalar", _ln2st[0])
                _ln2st[0] = op("scalar", lambda s, b=b: s.activation(
                    junk3, x_tm[:, b * DM:(b + 1) * DM], AF.Square,
                    accum_out=sca[:, C_LI + b:C_LI + b + 1]), inc=True)
        bwg.rdone(("p_t", B.sems["p_t"][1]))

    if DEBUG_TAPS:
        bx.rd("sync")
        dma("sync", dbg_x2, x_tm[:, :])
    B.join()
    stB3.close()
    stB.close()

    emit_layernorm(update_xa=False)

    stC = ExitStack()
    eC = stC.enter_context
    gsus = eC(nc.sbuf_tensor("gsus", [128, 2 * NIC], F32))
    dma("sync", gsus[:, 0:NIC], AP(gs_h.tensor, 0, [[1, 128], [128, NIC]]),
        allow_slow_non_contiguous=True)
    v_gsus = dma("sync", gsus[:, NIC:2 * NIC],
                 AP(us_h.tensor, 0, [[1, 128], [128, NIC]]),
                 cont=True, allow_slow_non_contiguous=True)
    fuse = eC(nc.sbuf_tensor("fuse", [128, NIC * 512], BF16))
    bfuse = B.buf(fuse, "fuse")

    wait("gpsimd", v_ag)
    wait("sync", v_ag)

    def ag_gu_w(base_off, w, c0, n):
        return AP(ag_out, base_off + w * 128 * I_FFN + c0,
                  [[I_FFN, 128], [SEC, 8], [1, n]])

    def ag_down_w(w, c0, n):
        return AP(ag_out, 2 * 256 * I_FFN + w * 128 * DM + c0,
                  [[DM, 128], [SEC, 8], [1, n]])

    gslab = eC(nc.sbuf_tensor("gslab", [128, 2 * NI * 256], F8))
    uslab = eC(nc.sbuf_tensor("uslab", [128, 2 * NI * 256], F8))
    sgf = eC(nc.sbuf_tensor("sgf", [128, 512], BF16))
    upf = eC(nc.sbuf_tensor("upf", [128, 512], BF16))
    dslab = eC(nc.sbuf_tensor("dslab", [128, 4 * 12 * 256], F8))
    oacc = eC(nc.sbuf_tensor("oacc", [128, 256], F32))
    ds_sl = eC(nc.sbuf_tensor("ds_sl", [128, 2 * 256], F32))
    bgsl2 = [B.buf(gslab, "gslab0"), B.buf(gslab, "gslab1")]
    busl2 = [B.buf(uslab, "uslab0"), B.buf(uslab, "uslab1")]
    bsgf = B.buf(sgf, "sgf")
    bupf = B.buf(upf, "upf")
    bdsl2 = [B.buf(dslab, "dslab0"), B.buf(dslab, "dslab1"),
             B.buf(dslab, "dslab2"), B.buf(dslab, "dslab3")]
    boa = B.buf(oacc, "oacc")
    bdss2 = [B.buf(ds_sl, "ds_sl0"), B.buf(ds_sl, "ds_sl1")]
    n_out = 0
    for sh in range(2):
        for og2 in range(24):
            sp = og2 % 2
            so = sp * NI * 256
            bgsl2[sp].wr("sync")
            for w in range(2):
                v_g1 = dma("sync",
                           AP(gslab, so + w * 256, [[2 * NI * 256, 128], [512, 8], [1, 256]]),
                           ag_gu_w(0, w, og2 * 256, 256), cont=(w > 0))
            busl2[sp].wr("sync")
            for w in range(2):
                v_sl = dma("sync",
                           AP(uslab, so + w * 256, [[2 * NI * 256, 128], [512, 8], [1, 256]]),
                           ag_gu_w(256 * I_FFN, w, og2 * 256, 256), cont=(w > 0))
            bgsl2[sp].wdone(v_g1)
            busl2[sp].wdone(v_sl)
            bgsl2[sp].rd("tensor")
            busl2[sp].rd("tensor")
            bln.rd("tensor")
            for oc in range(2):
                ic = og2 * 2 + oc
                fb0 = ((og2 * 2 + oc) % 2) * 2
                bbuf[fb0].wr("tensor")
                bbuf[fb0 + 1].wr("tensor")
                for i in range(NI):
                    op("tensor", lambda t, i=i, oc=oc, sh=sh, so=so, fb0=fb0:
                       t.matmul(
                        banks[fb0][:, :],
                        gslab[:, so + i * 256 + oc * 128:so + i * 256 + (oc + 1) * 128],
                        lnXt[:, i * S + sh * 512:i * S + (sh + 1) * 512],
                        start=(i == 0), stop=(i == NI - 1)))
                    op("tensor", lambda t, i=i, oc=oc, sh=sh, so=so, fb0=fb0:
                       t.matmul(
                        banks[fb0 + 1][:, :],
                        uslab[:, so + i * 256 + oc * 128:so + i * 256 + (oc + 1) * 128],
                        lnXt[:, i * S + sh * 512:i * S + (sh + 1) * 512],
                        start=(i == 0), stop=(i == NI - 1)))
                v_fmm = mark("tensor")
                bbuf[fb0].wdone(v_fmm)
                bbuf[fb0 + 1].wdone(v_fmm)
                wait("scalar", v_gsus)
                bsgf.wr("scalar")
                bupf.wr("scalar")
                bbuf[fb0].rd("scalar")
                bbuf[fb0 + 1].rd("scalar")
                op("scalar", lambda s, ic=ic, fb0=fb0: s.activation(
                    sgf[:, :], banks[fb0][:, :], AF.Silu, scale=gsus[:, ic:ic + 1]))
                v_upf = op("scalar", lambda s, ic=ic, fb0=fb0: s.activation(
                    upf[:, :], banks[fb0 + 1][:, :], AF.Identity,
                    scale=gsus[:, NIC + ic:NIC + ic + 1]), inc=True)
                bbuf[fb0].rdone(v_upf)
                bbuf[fb0 + 1].rdone(v_upf)
                bsgf.wdone(v_upf)
                bupf.wdone(v_upf)
                bfuse.wr("vector")
                bsgf.rd("vector")
                bupf.rd("vector")
                v_fu = op("vector", lambda v, ic=ic: v.tensor_mul(
                    fuse[:, ic * 512:(ic + 1) * 512], sgf[:, :], upf[:, :]),
                    inc=True)
                bsgf.rdone(v_fu)
                bupf.rdone(v_fu)
                if og2 == 23 and oc == 1:
                    bfuse.wdone(v_fu)
            bgsl2[sp].rdone(("p_t", B.sems["p_t"][1]))
            busl2[sp].rdone(("p_t", B.sems["p_t"][1]))

        for og in range(8):
            dp_ = og % 2
            bdss2[dp_].wr("sync")
            v_dss = dma("sync", ds_sl[:, dp_ * 256:(dp_ + 1) * 256],
                        AP(ds_h.tensor, og * 256, [[0, 128], [1, 256]]))
            bdss2[dp_].wdone(v_dss)
            for bb in range(4):
                bbuf[2 + bb].wr("tensor")
            bfuse.rd("tensor")
            v_pieces = []
            for pc in range(4):
                pp = (og * 4 + pc) % 4
                po = pp * 12 * 256
                bdsl2[pp].wr("sync")
                v_dd = None
                for w6 in range(6):
                    v_dd = dma("sync",
                               AP(dslab, po + w6 * 256,
                                  [[4 * 12 * 256, 128], [6 * 256, 2], [1, 256]]),
                               AP(ag_out, 2 * 256 * I_FFN + (2 * pc) * SEC
                                  + w6 * 128 * DM + og * 256,
                                  [[DM, 128], [SEC, 2], [1, 256]]),
                               cont=(w6 > 0))
                bdsl2[pp].wdone(v_dd)
                bdsl2[pp].rd("tensor")
                for jj in range(12):
                    j = pc * 12 + jj
                    lc = po + jj * 256
                    for bb in range(4):
                        b = sh * 4 + bb
                        op("tensor", lambda t, j=j, jj=jj, bb=bb, lc=lc: t.matmul(
                            banks[2 + bb][:, 0:256],
                            fuse[:, j * 512 + bb * 128:j * 512 + (bb + 1) * 128],
                            dslab[:, lc:lc + 256], start=(j == 0), stop=(j == NIC - 1)))
                v_pc = mark("tensor")
                bdsl2[pp].rdone(v_pc)
            v_dmm2 = ("p_t", B.sems["p_t"][1])
            for bb in range(4):
                b = sh * 4 + bb
                bbuf[2 + bb].wdone(v_dmm2)
                bdss2[dp_].rd("vector")
                boa.wr("vector")
                bbuf[2 + bb].rd("vector")
                B.sop("vector", lambda v, bb=bb, dp_=dp_: v.tensor_mul(
                    oacc[:, :], banks[2 + bb][:, 0:256],
                    ds_sl[:, dp_ * 256:(dp_ + 1) * 256]))
                bx.wr("vector")
                v_oadd = B.sop("vector", lambda v, b=b, og=og: v.tensor_add(
                    x_tm[:, b * DM + og * 256:b * DM + (og + 1) * 256],
                    x_tm[:, b * DM + og * 256:b * DM + (og + 1) * 256],
                    oacc[:, :]))
                bbuf[2 + bb].rdone(v_oadd)
                boa.wdone(v_oadd)
                bdss2[dp_].rdone(v_oadd)
                bx.wdone(v_oadd)
        bx.rd("sync")
        for bb in range(4):
            b = sh * 4 + bb
            s = B.sems["o_d"]
            s[1] += 16
            h_, v_ = s[0], s[1]
            B.Q["sync"].append(lambda E, b=b, h_=h_: E.dma_start(
                out=out_h[b * 128:(b + 1) * 128, :],
                in_=x_tm[:, b * DM:(b + 1) * DM]).then_inc(h_, 16))
            n_out += 16
        bx.rdone(("o_d", B.sems["o_d"][1]))

    B.raw("sync", lambda E, v=n_out: E.wait_ge(B.sems["o_d"][0], v))

    with nc.Block() as block:
        @block.sync
        def _(E):
            for fn in B.Q["sync"]:
                fn(E)

        @block.tensor
        def _(E):
            for fn in B.Q["tensor"]:
                fn(E)

        @block.vector
        def _(E):
            for fn in B.Q["vector"]:
                fn(E)

        @block.scalar
        def _(E):
            for fn in B.Q["scalar"]:
                fn(E)

        @block.gpsimd
        def _(E):
            for fn in B.Q["gpsimd"]:
                fn(E)

    stC.close()
    B.st.close()
    return nc


def kernel(**inputs):
    gi = lambda k: np.asarray(inputs[k])
    x = gi("L0_attn_hidden_states").reshape(S_TOT, DM)
    wq = gi("L0_attn_Wq"); bq = gi("L0_attn_bq")
    wk = gi("L0_attn_Wk"); bk = gi("L0_attn_bk")
    wv = gi("L0_attn_Wv"); bv = gi("L0_attn_bv")
    wg = gi("L0_attn_Wg"); bg = gi("L0_attn_bg")
    wmb = gi("L0_attn_Wmb"); bmb = gi("L0_attn_bmb")
    mem = gi("L0_attn_memory"); nrm = gi("L0_attn_norm")
    gw = gi("L0_ffn_gate_w"); uw = gi("L0_ffn_up_w"); dw = gi("L0_ffn_down_w")
    gsc = gi("L0_ffn_gate_scale"); usc = gi("L0_ffn_up_scale")
    dsc = gi("L0_ffn_down_scale")

    wq_pre = np.ascontiguousarray(
        wq.T.reshape(NI, 128, NH, 128).transpose(2, 1, 0, 3)).reshape(
        NH, 128, NI * 128)
    wk_pre = np.ascontiguousarray(
        wk.T.reshape(NI, 128, NKV, 128).transpose(2, 1, 0, 3)).reshape(
        NKV, 128, NI * 128)
    wg_pre = np.ascontiguousarray(
        wg.T.reshape(NI, 128, 4, 512).transpose(2, 1, 0, 3)).reshape(
        4, 128, NI * 512)
    wmbv_pre = np.ascontiguousarray(np.concatenate([
        wmb.T.reshape(NI, 128, NKV, 512).transpose(2, 1, 0, 3),
        wv.T.reshape(NI, 128, NKV, 128).transpose(2, 1, 0, 3)],
        axis=3)).reshape(NKV, 128, NI * 640)
    memnorm = np.concatenate([mem, nrm[:, :, None]], axis=2)
    memnorm_f = np.ascontiguousarray(memnorm.transpose(1, 0, 2)).reshape(F, NH * 129)
    mn_pre = np.ascontiguousarray(
        memnorm_f.reshape(NF, 128, 4, 2, 2, 129).transpose(
            2, 1, 3, 0, 4, 5)).reshape(NKV, 128, 2 * NF * 258)
    mn_seq = np.ascontiguousarray(
        memnorm_f.reshape(NF, 128, NH * 129).transpose(1, 0, 2)).reshape(
        128, NF * 2064)

    common = dict(wqT=wq_pre, wkT=wk_pre, wgT=wg_pre, wmbv=wmbv_pre,
                  bq=bq, bk=bk, bv=bv, bg=bg, bmb=bmb,
                  mn_pre=mn_pre, mn_seq=mn_seq, gate_scale=gsc, up_scale=usc,
                  down_scale=dsc)

    in_maps = []
    for c in range(NCORES):
        wslice = np.concatenate([
            gw[c * 256:(c + 1) * 256, :].reshape(-1),
            uw[c * 256:(c + 1) * 256, :].reshape(-1),
            dw[c * 768:(c + 1) * 768, :].reshape(-1),
        ])
        m = dict(common)
        m["x"] = np.ascontiguousarray(x[c * S:(c + 1) * S, :])
        m["wslice"] = np.ascontiguousarray(wslice)
        in_maps.append(m)

    nc = build_nc()
    res = run_bass_kernel_spmd(nc, in_maps, core_ids=list(range(NCORES)))
    out = np.concatenate([np.asarray(r["out"]) for r in res.results], axis=0)
    return out.reshape(1, S_TOT, DM).astype(np.float32, copy=False)


# version 24
# speedup vs baseline: 1.1764x; 1.1764x over previous
import numpy as np
from contextlib import ExitStack

import concourse.bass as bass
import concourse.mybir as mybir
from concourse.bass_utils import run_bass_kernel_spmd
from concourse.ap import AP

F32 = mybir.dt.float32
BF16 = mybir.dt.bfloat16
F8 = mybir.dt.float8e4
AF = mybir.ActivationFunctionType
ALU = mybir.AluOpType

DEBUG_TAPS = False
S_TOT = 8192
NCORES = 8
S = S_TOT // NCORES
NB = S // 128
DM = 2048
NI = DM // 128
NKV = 4
NH = 16
F = 768
NF = F // 128
I_FFN = 6144
NIC = I_FFN // 128
EPS = 1e-8
LN_EPS = 1e-5
SEC = 2 * 256 * I_FFN + 768 * DM
QT = SEC // (128 * 512)


class Builder:
    def __init__(self):
        self.nc = bass.Bass("TRN2", target_bir_lowering=False, debug=False)
        self.Q = {"sync": [], "tensor": [], "vector": [], "scalar": [], "gpsimd": []}
        self.last_op_idx = {e: None for e in self.Q}
        self.ops_since_inc = {e: 0 for e in self.Q}
        self.last_inc_ev = {e: None for e in self.Q}
        self.st = ExitStack()
        nc = self.nc
        e = self.st.enter_context
        self.sems = {}
        for name in ("p_t", "p_v", "p_s", "p_g", "o_d", "cc"):
            self.sems[name] = [e(nc.semaphore(name)), 0]
        self.ENG_SEM = {"tensor": "p_t", "vector": "p_v", "scalar": "p_s",
                        "gpsimd": "p_g"}
        self.dma_pool = {"sync": [], "gpsimd": [], "scalar": []}
        for qn, n in (("sync", 16), ("gpsimd", 12), ("scalar", 8)):
            for i in range(n):
                nm = f"dq_{qn}_{i}"
                self.sems[nm] = [e(nc.semaphore(nm)), 0]
                self.dma_pool[qn].append(nm)
        self.pool_idx = {"sync": 0, "gpsimd": 0, "scalar": 0}
        self.cur_grp = {"sync": None, "gpsimd": None, "scalar": None}
        self.waited = {eng: {} for eng in self.Q}

    def raw(self, eng, fn):
        self.Q[eng].append(fn)

    def op(self, eng, fn, inc=False):
        if inc:
            s = self.sems[self.ENG_SEM[eng]]
            s[1] += 1
            h, v = s[0], s[1]
            self.Q[eng].append(lambda E, fn=fn, h=h: fn(E).then_inc(h, 1))
            self.last_op_idx[eng] = len(self.Q[eng]) - 1
            self.ops_since_inc[eng] = 0
            self.last_inc_ev[eng] = (self.ENG_SEM[eng], v)
            return self.last_inc_ev[eng]
        self.Q[eng].append(fn)
        self.last_op_idx[eng] = len(self.Q[eng]) - 1
        self.ops_since_inc[eng] += 1
        return None

    def op_cc(self, eng, fn):
        s = self.sems["cc"]
        s[1] += 1
        h, v = s[0], s[1]
        self.Q[eng].append(lambda E, fn=fn, h=h: fn(E).then_inc(h, 1))
        return ("cc", v)

    def fence_evs(self):
        evs = [(n, self.sems[n][1]) for n in ("p_t", "p_v", "p_s")]
        for qn in self.dma_pool:
            for nm in self.dma_pool[qn]:
                if self.sems[nm][1] > 0:
                    evs.append((nm, self.sems[nm][1]))
        return evs

    def mark(self, eng):
        if self.ops_since_inc[eng] == 0 and self.last_inc_ev[eng] is not None:
            return self.last_inc_ev[eng]
        i = self.last_op_idx[eng]
        assert i is not None, f"mark({eng}) with no prior datapath op"
        s = self.sems[self.ENG_SEM[eng]]
        s[1] += 1
        h, v = s[0], s[1]
        old = self.Q[eng][i]
        self.Q[eng][i] = lambda E, old=old, h=h: old(E).then_inc(h, 1)
        self.ops_since_inc[eng] = 0
        self.last_inc_ev[eng] = (self.ENG_SEM[eng], v)
        return self.last_inc_ev[eng]

    def sop(self, eng, fn):
        self.waitf(eng, self.last_inc_ev[eng])
        return self.op(eng, fn, inc=True)

    def dma(self, eng, out, in_, cont=False, **kw):
        if not cont or self.cur_grp[eng] is None:
            i = self.pool_idx[eng]
            pool = self.dma_pool[eng]
            self.pool_idx[eng] = (i + 1) % len(pool)
            nm = pool[i]
            st = self.sems[nm]
            if st[1] > 0:
                self.waitf(eng, (nm, st[1]))
            self.cur_grp[eng] = nm
        nm = self.cur_grp[eng]
        st = self.sems[nm]
        st[1] += 16
        h, v = st[0], st[1]
        self.Q[eng].append(lambda E, out=out, in_=in_, kw=kw, h=h:
                           E.dma_start(out=out, in_=in_, **kw).then_inc(h, 16))
        return (nm, v)

    def wait_all_dma(self, eng, queue):
        for nm in self.dma_pool[queue]:
            st = self.sems[nm]
            if st[1] > 0:
                self.waitf(eng, (nm, st[1]))

    def wait(self, eng, ev):
        if ev is None:
            return
        sem_name, val = ev
        if self.ENG_SEM.get(eng) == sem_name:
            return
        cur = self.waited[eng].get(sem_name, 0)
        if val <= cur:
            return
        self.waited[eng][sem_name] = val
        h = self.sems[sem_name][0]
        self.Q[eng].append(lambda E, h=h, v=val: E.wait_ge(h, v))

    def waitf(self, eng, ev):
        if ev is None:
            return
        sem_name, val = ev
        cur = self.waited[eng].get(sem_name, 0)
        if val <= cur:
            return
        self.waited[eng][sem_name] = val
        h = self.sems[sem_name][0]
        self.Q[eng].append(lambda E, h=h, v=val: E.wait_ge(h, v))

    def join(self):
        evs = [self.mark(eng) for eng in ("tensor", "vector", "scalar", "gpsimd")]
        for qn in self.dma_pool:
            for nm in self.dma_pool[qn]:
                st = self.sems[nm]
                if st[1] > 0:
                    evs.append((nm, st[1]))
        for eng in self.Q:
            for ev in evs:
                self.waitf(eng, ev) if ev[0].startswith("dq_") else self.wait(eng, ev)

    class Buf:
        def __init__(self, bld, handle, name):
            self.b = bld
            self.h = handle
            self.name = name
            self.w_ev = None
            self.w_eng = None
            self.r_evs = []

        def rd(self, eng):
            self.b.waitf(eng, self.w_ev)

        def rdone(self, ev):
            if ev is not None:
                self.r_evs.append(ev)

        def wr(self, eng):
            self.b.waitf(eng, self.w_ev)
            for ev in self.r_evs:
                self.b.waitf(eng, ev)

        def wdone(self, ev, eng=None):
            self.w_ev = ev
            self.r_evs = []

        def seed(self, evs):
            self.r_evs.extend(evs)

    def buf(self, handle, name):
        return Builder.Buf(self, handle, name)


def build_nc():
    B = Builder()
    nc = B.nc
    e = B.st.enter_context
    op, dma, wait, mark = B.op, B.dma, B.wait, B.mark

    x_in = nc.dram_tensor("x", [S, DM], F32, kind="ExternalInput").ap()
    wqT = nc.dram_tensor("wqT", [NH, 128, NI * 128], F32, kind="ExternalInput").ap()
    wkT = nc.dram_tensor("wkT", [NKV, 128, NI * 128], F32, kind="ExternalInput").ap()
    wgT = nc.dram_tensor("wgT", [4, 128, NI * 512], F32, kind="ExternalInput").ap()
    wmbv_h = nc.dram_tensor("wmbv", [NKV, 128, NI * 640], F32,
                            kind="ExternalInput").ap()
    bq_h = nc.dram_tensor("bq", [DM], F32, kind="ExternalInput").ap()
    bk_h = nc.dram_tensor("bk", [512], F32, kind="ExternalInput").ap()
    bv_h = nc.dram_tensor("bv", [512], F32, kind="ExternalInput").ap()
    bg_h = nc.dram_tensor("bg", [DM], F32, kind="ExternalInput").ap()
    bmb_h = nc.dram_tensor("bmb", [DM], F32, kind="ExternalInput").ap()
    mn_pre = nc.dram_tensor("mn_pre", [NKV, 128, 2 * NF * 258], F32,
                            kind="ExternalInput").ap()
    mn_seq = nc.dram_tensor("mn_seq", [128, NF * 2064], F32,
                            kind="ExternalInput").ap()
    wsl_h = nc.dram_tensor("wslice", [SEC], F32, kind="ExternalInput").ap()
    gs_h = nc.dram_tensor("gate_scale", [I_FFN], F32, kind="ExternalInput").ap()
    us_h = nc.dram_tensor("up_scale", [I_FFN], F32, kind="ExternalInput").ap()
    ds_h = nc.dram_tensor("down_scale", [DM], F32, kind="ExternalInput").ap()
    out_h = nc.dram_tensor("out", [S, DM], F32, kind="ExternalOutput").ap()

    dbg_memh = (nc.dram_tensor("dbg_memh", [128, NB * DM], BF16).ap()
                if DEBUG_TAPS else None)
    dbg_x2 = (nc.dram_tensor("dbg_x2", [128, NB * DM], F32).ap()
              if DEBUG_TAPS else None)
    dbg_nm = (nc.dram_tensor("dbg_nm", [128, NF * NH * 129], BF16).ap()
              if DEBUG_TAPS else None)
    ard_in = nc.dram_tensor("ard_in", [F, NH * 128], BF16)
    ard_out = nc.dram_tensor("ard_out", [F, NH * 128], BF16)
    arn_in = nc.dram_tensor("arn_in", [NH, F], BF16)
    arn_out = nc.dram_tensor("arn_out", [NH, F], BF16)
    ag_in = nc.dram_tensor("ag_in", [SEC], F8)
    ag_out = nc.dram_tensor("ag_out", [NCORES * SEC], F8)

    banks = [e(nc.psum_tensor(f"bank{i}", [128, 512], F32)) for i in range(8)]
    bbuf = [B.buf(banks[i], f"bank{i}") for i in range(8)]

    x_tm = e(nc.sbuf_tensor("x_tm", [128, NB * DM], F32))
    lnXt = e(nc.sbuf_tensor("lnXt", [128, NI * S], BF16))
    ln_sb = e(nc.sbuf_tensor("ln_sb", [128, DM], BF16))
    sca = e(nc.sbuf_tensor("sca", [128, 288], F32))
    ident = e(nc.sbuf_tensor("ident", [128, 128], BF16))
    rollm = e(nc.sbuf_tensor("rollm", [128, 6 * 128], BF16))
    ones_col = e(nc.sbuf_tensor("ones_col", [128, 1], BF16))
    ones_row = e(nc.sbuf_tensor("ones_row", [1, 128], BF16))
    ones_f32 = e(nc.sbuf_tensor("ones_f32", [1, 1], F32))
    ones_bf = e(nc.sbuf_tensor("ones_bf", [1, 1], BF16))
    bkq = e(nc.sbuf_tensor("bkq", [128, 2 * (NKV + NH)], F32))
    coefb = e(nc.sbuf_tensor("coefb", [128, 32], BF16))
    cst = e(nc.sbuf_tensor("cst", [128, 2], F32))

    bx = B.buf(x_tm, "x_tm")
    bln = B.buf(lnXt, "lnXt")
    blnsb = [B.buf(ln_sb, "ln_sb")]
    bcoef = B.buf(coefb, "coefb")

    C_LM, C_LI, C_LMI, C_L1I, C_NMI = 0, 8, 16, 24, 32
    C_SSQ, C_N, C_EN, C_INVN, C_RSSQ = 40, 48, 56, 64, 72
    C_RK, C_DPV, C_TMUL, C_NC1, C_COEF = 80, 112, 144, 176, 208
    C_QSSQ, C_QEN, C_NAW = 80, 96, 112
    C_TMP = 256

    v0 = op("gpsimd", lambda g: g.memset(ident[:, :], 0.0), inc=True)
    B.waitf("gpsimd", v0)
    v_id = op("gpsimd", lambda g: g.affine_select(
        out=ident[:, :], in_=ident[:, :], compare_op=ALU.not_equal, fill=1.0,
        base=0, pattern=[[-1, 128]], channel_multiplier=1), inc=True)
    v0r = op("gpsimd", lambda g: g.memset(rollm[:, :], 0.0), inc=True)
    B.waitf("gpsimd", v0r)
    v_roll = None
    for j_ in (1, 2, 3):
        op("gpsimd", lambda g, j_=j_: g.affine_select(
            out=rollm[:, (j_ - 1) * 128:j_ * 128],
            in_=rollm[:, (j_ - 1) * 128:j_ * 128], compare_op=ALU.not_equal,
            fill=1.0, base=j_, pattern=[[-1, 128]], channel_multiplier=1))
        v_roll = op("gpsimd", lambda g, j_=j_: g.affine_select(
            out=rollm[:, (2 + j_) * 128:(3 + j_) * 128],
            in_=rollm[:, (2 + j_) * 128:(3 + j_) * 128],
            compare_op=ALU.not_equal, fill=1.0, base=j_ - 128,
            pattern=[[-1, 128]], channel_multiplier=1), inc=True)
        B.waitf("gpsimd", v_roll)
    op("gpsimd", lambda g: g.memset(cst[:, 0:1], float(LN_EPS)), inc=True)
    v_cst = op("gpsimd", lambda g: g.memset(cst[:, 1:2], float(EPS * EPS)), inc=True)
    op("gpsimd", lambda g: g.memset(ones_col[:, :], 1.0), inc=True)
    op("gpsimd", lambda g: g.memset(ones_row[:, :], 1.0), inc=True)
    op("gpsimd", lambda g: g.memset(ones_f32[:, :], 1.0), inc=True)
    v_ones = op("gpsimd", lambda g: g.memset(ones_bf[:, :], 1.0), inc=True)
    v_cst = v_ones
    v_id = ("p_g", v_ones[1]) if False else v_id

    dma("sync", bkq[:, 0:NKV], AP(bk_h.tensor, 0, [[1, 128], [128, NKV]]),
        allow_slow_non_contiguous=True)
    v_bkq0 = dma("sync", bkq[:, NKV:NKV + NH],
                 AP(bq_h.tensor, 0, [[1, 128], [128, NH]]),
                 cont=True, allow_slow_non_contiguous=True)
    wait("vector", v_bkq0)
    v_nbias = op("vector", lambda v: v.tensor_scalar_mul(
        bkq[:, NKV + NH:2 * (NKV + NH)], bkq[:, 0:NKV + NH], -1.0), inc=True)

    bx.wr("sync")
    vx = [dma("sync", x_tm[:, b * DM:(b + 1) * DM], x_in[b * 128:(b + 1) * 128, :])
          for b in range(NB)]
    bx.wdone(vx[-1])

    stQ = ExitStack()
    q8 = stQ.enter_context(nc.sbuf_tensor("q8", [128, 2 * 512], F8))

    _qst = {"st": [None, None]}

    def emit_quant_tiles(k0, k1):
        for k in range(k0, k1):
            p = k % 2
            wait("gpsimd", _qst["st"][p])
            v8 = dma("gpsimd", q8[:, p * 512:(p + 1) * 512],
                     AP(wsl_h.tensor, k * 128 * 512, [[512, 128], [1, 512]]))
            wait("sync", v8)
            _qst["st"][p] = dma("sync",
                                AP(ag_in, k * 128 * 512, [[512, 128], [1, 512]]),
                                q8[:, p * 512:(p + 1) * 512])

    def emit_ag():
        wait("gpsimd", _qst["st"][0])
        wait("gpsimd", _qst["st"][1])
        B.wait_all_dma("gpsimd", "sync")
        return B.op_cc("gpsimd", lambda g: g.collective_compute(
            "AllGather", ALU.bypass, replica_groups=[list(range(NCORES))],
            ins=[ag_in.ap()], outs=[ag_out.ap()]))

    def emit_layernorm(update_xa, v_stats_pre=None):
        jb = blnsb[0]
        junk = ln_sb[:, :]
        if v_stats_pre is not None:
            v_stats = v_stats_pre
        else:
            ev = None
            for b in range(NB):
                xb = x_tm[:, b * DM:(b + 1) * DM]
                bx.rd("scalar")
                if b == 0:
                    jb.wr("scalar")
                else:
                    B.waitf("scalar", ev)
                ev = op("scalar", lambda s, xb=xb, b=b: s.activation(
                    junk, xb, AF.Identity, accum_out=sca[:, C_LM + b:C_LM + b + 1]),
                    inc=True)
                B.waitf("scalar", ev)
                ev = op("scalar", lambda s, xb=xb, b=b: s.activation(
                    junk, xb, AF.Square, accum_out=sca[:, C_LI + b:C_LI + b + 1]),
                    inc=True)
            v_stats = ev
        wait("vector", v_stats)
        B.sop("vector", lambda v: v.tensor_scalar_mul(
            sca[:, C_LM:C_LM + 8], sca[:, C_LM:C_LM + 8], 1.0 / DM))
        B.sop("vector", lambda v: v.tensor_scalar_mul(
            sca[:, C_LI:C_LI + 8], sca[:, C_LI:C_LI + 8], 1.0 / DM))
        B.sop("vector", lambda v: v.tensor_mul(
            sca[:, C_LMI:C_LMI + 8], sca[:, C_LM:C_LM + 8], sca[:, C_LM:C_LM + 8]))
        vv = B.sop("vector", lambda v: v.tensor_sub(
            sca[:, C_LI:C_LI + 8], sca[:, C_LI:C_LI + 8], sca[:, C_LMI:C_LMI + 8]))
        wait("scalar", vv)
        wait("scalar", v_cst)
        B.waitf("scalar", v_stats)
        vstd = op("scalar", lambda s: s.activation(
            sca[:, C_L1I:C_L1I + 8], sca[:, C_LI:C_LI + 8], AF.Sqrt,
            bias=cst[:, 0:1]), inc=True)
        wait("vector", vstd)
        B.sop("vector", lambda v: v.reciprocal(sca[:, C_LI:C_LI + 8],
                                               sca[:, C_L1I:C_L1I + 8]))
        B.sop("vector", lambda v: v.tensor_mul(
            sca[:, C_LMI:C_LMI + 8], sca[:, C_LM:C_LM + 8], sca[:, C_LI:C_LI + 8]))
        B.sop("vector", lambda v: v.tensor_scalar_mul(
            sca[:, C_NMI:C_NMI + 8], sca[:, C_LMI:C_LMI + 8], -1.0))
        vmi = B.sop("vector", lambda v: v.tensor_scalar_add(
            sca[:, C_L1I:C_L1I + 8], sca[:, C_LI:C_LI + 8], 1.0))
        wait("scalar", vmi)
        for b in range(NB):
            xb = x_tm[:, b * DM:(b + 1) * DM]
            bx.rd("scalar")
            jb.wr("scalar")
            vln = op("scalar", lambda s, xb=xb, b=b: s.activation(
                junk, xb, AF.Identity, bias=sca[:, C_NMI + b:C_NMI + b + 1],
                scale=sca[:, C_LI + b:C_LI + b + 1]), inc=True)
            jb.wdone(vln)
            if update_xa:
                bx.wr("vector")
                wait("vector", vln)
                B.waitf("vector", vmi)
                vxa = op("vector", lambda v, xb=xb, b=b: v.tensor_scalar(
                    xb, xb, sca[:, C_L1I + b:C_L1I + b + 1],
                    sca[:, C_LMI + b:C_LMI + b + 1], ALU.mult, ALU.subtract),
                    inc=True)
                bx.wdone(vxa)
            jb.rd("tensor")
            wait("tensor", v_id)
            for half in range(2):
                tb = banks[4].ap().bitcast(BF16)
                bbuf[4].wr("tensor")
                for ii in range(8):
                    i = half * 8 + ii
                    op("tensor", lambda t, junk=junk, i=i, ii=ii, tb=tb: t.transpose(
                        tb[:, ii * 128:(ii + 1) * 128],
                        junk[:, i * 128:(i + 1) * 128], ident[:, :]))
                vt = mark("tensor")
                bbuf[4].wdone(vt)
                bln.wr("vector")
                bbuf[4].rd("vector")
                dst = AP(lnXt, (half * 8) * S + b * 128,
                         [[NI * S, 128], [S, 8], [1, 128]])
                vev = op("vector", lambda v, tb=tb, dst=dst: v.tensor_copy(
                    dst, tb[:, :].rearrange("p (c n) -> p c n", c=8)), inc=True)
                bbuf[4].rdone(vev)
                bln.wdone(vev)
            jb.rdone(mark("tensor"))

    for b in range(NB):
        wait("scalar", vx[b])
    emit_layernorm(update_xa=True)
    v_ln1 = ("p_v", B.sems["p_v"][1])

    stA = ExitStack()
    eA = stA.enter_context
    wk_g = eA(nc.sbuf_tensor("wk_g", [128, NI * 128], BF16))
    brow = eA(nc.sbuf_tensor("brow", [1, 512 + DM], BF16))
    wmbv_g = eA(nc.sbuf_tensor("wmbv_g", [128, NI * 640], BF16))
    mn_g = eA(nc.sbuf_tensor("mn_g", [128, 2 * NF * 258], BF16))
    xcat_k = eA(nc.sbuf_tensor("xcat_k", [128, 2 * S], BF16))
    xcat_tm = eA(nc.sbuf_tensor("xcat_tm", [128, NB * 259], BF16))
    dpk_fm = eA(nc.sbuf_tensor("dpk_fm", [128, NF * S], BF16))
    dpk_tm = eA(nc.sbuf_tensor("dpk_tm", [128, NB * F], BF16))
    mb_g = eA(nc.sbuf_tensor("mb_g", [128, NB * 512], BF16))
    v_g = eA(nc.sbuf_tensor("v_g", [128, NB * 128], BF16))
    pv_raw = eA(nc.sbuf_tensor("pv_raw", [128, NB * 516], BF16))
    wmv_h = eA(nc.sbuf_tensor("wmv_h", [128, NB * 128], BF16))
    dstage = eA(nc.sbuf_tensor("dstage", [128, F], BF16))

    dma("gpsimd", brow[0:1, 0:512], bv_h.unsqueeze(0))
    v_brow = dma("gpsimd", brow[0:1, 512:512 + DM], bmb_h.unsqueeze(0), cont=True)
    bwk = B.buf(wk_g, "wk_g")
    bwmbv = B.buf(wmbv_g, "wmbv_g")
    bmn = B.buf(mn_g, "mn_g")
    bxck = B.buf(xcat_k, "xcat_k")
    bxctm = B.buf(xcat_tm, "xcat_tm")
    bdpf = B.buf(dpk_fm, "dpk_fm")
    bdpt = B.buf(dpk_tm, "dpk_tm")
    bmb = B.buf(mb_g, "mb_g")
    bv_ = B.buf(v_g, "v_g")
    bvn = bv_
    bpv = B.buf(pv_raw, "pv_raw")
    bwmv = B.buf(wmv_h, "wmv_h")
    bds = B.buf(dstage, "dstage")
    bdn = bds
    bsca = B.buf(sca, "sca")

    dstage2 = eA(nc.sbuf_tensor("dstage2", [128, F], BF16))
    bds2 = B.buf(dstage2, "dstage2")

    for g in range(NKV):
        bwk.wr("gpsimd")
        v_wk = dma("gpsimd", wk_g[:, :],
                   AP(wkT.tensor, g * 128 * NI * 128,
                      [[NI * 128, 128], [1, NI * 128]]))
        bwk.wdone(v_wk)
        bwmbv.wr("gpsimd")
        v_wv = dma("gpsimd", wmbv_g[:, :],
                   AP(wmbv_h.tensor, g * 128 * NI * 640,
                      [[NI * 640, 128], [1, NI * 640]]))
        bwmbv.wdone(v_wv)
        bmn.wr("gpsimd")
        v_mn = dma("gpsimd", mn_g[:, :],
                   AP(mn_pre.tensor, g * 128 * 2 * NF * 258,
                      [[2 * NF * 258, 128], [1, 2 * NF * 258]]))
        bmn.wdone(v_mn)
        emit_quant_tiles(g * 18, (g + 1) * 18)

        bwk.rd("tensor")
        bln.rd("tensor")
        wait("tensor", v_ln1)
        bbuf[0].wr("tensor")
        bbuf[1].wr("tensor")
        for i in range(NI):
            for h2 in range(2):
                op("tensor", lambda t, i=i, h2=h2: t.matmul(
                    banks[h2][:, :], wk_g[:, i * 128:(i + 1) * 128],
                    lnXt[:, i * S + h2 * 512:i * S + (h2 + 1) * 512],
                    start=(i == 0), stop=(i == NI - 1)))
        v_kmm = mark("tensor")
        bwk.rdone(v_kmm)
        bbuf[0].wdone(v_kmm)
        bbuf[1].wdone(v_kmm)

        wait("scalar", v_nbias)
        bxck.wr("scalar")
        bbuf[0].rd("scalar")
        bbuf[1].rd("scalar")
        for h2 in range(2):
            op("scalar", lambda s, h2=h2, g=g: s.activation(
                xcat_k[:, h2 * 512:(h2 + 1) * 512], banks[h2][:, :], AF.Relu,
                bias=bkq[:, g:g + 1]))
            op("scalar", lambda s, h2=h2, g=g: s.activation(
                xcat_k[:, S + h2 * 512:S + (h2 + 1) * 512], banks[h2][:, :],
                AF.Relu, bias=bkq[:, NKV + NH + g:NKV + NH + g + 1], scale=-1.0))
        v_kevac = mark("scalar")
        bxck.wdone(v_kevac)
        bbuf[0].rdone(v_kevac)
        bbuf[1].rdone(v_kevac)

        wait("tensor", v_roll)
        bdpf.wr("vector")
        v_dpfm = None
        for j in (1, 2, 3):
            bxck.rd("tensor")
            for half in range(2):
                bks = (2, 3) if half == 0 else (0, 1)
                for c2 in range(2):
                    bbuf[bks[c2]].wr("tensor")
                    op("tensor", lambda t, j=j, half=half, c2=c2, bks=bks: t.matmul(
                        banks[bks[c2]][:, :],
                        rollm[:, (j - 1) * 128:j * 128],
                        xcat_k[:, half * S + c2 * 512:half * S + (c2 + 1) * 512],
                        start=True, stop=False))
                    op("tensor", lambda t, j=j, half=half, c2=c2, bks=bks: t.matmul(
                        banks[bks[c2]][:, :],
                        rollm[:, (2 + j) * 128:(3 + j) * 128],
                        xcat_k[:, (1 - half) * S + c2 * 512:
                               (1 - half) * S + (c2 + 1) * 512],
                        start=False, stop=True))
            v_shmm = mark("tensor")
            for bk in (0, 1, 2, 3):
                bbuf[bk].wdone(v_shmm)
            bxck.rdone(v_shmm)
            wait("vector", v_shmm)
            bxck.rd("vector")
            for half in range(2):
                bks = (2, 3) if half == 0 else (0, 1)
                t6 = (j - 1) * 2 + half
                for c2 in range(2):
                    bbuf[bks[c2]].rd("vector")
                    v_dpfm = op("vector", lambda v, t6=t6, half=half, c2=c2,
                                bks=bks: v.tensor_mul(
                        dpk_fm[:, t6 * S + c2 * 512:t6 * S + (c2 + 1) * 512],
                        xcat_k[:, half * S + c2 * 512:half * S + (c2 + 1) * 512],
                        banks[bks[c2]][:, :]), inc=True)
            for bk in (0, 1, 2, 3):
                bbuf[bk].rdone(v_dpfm)
            bxck.rdone(v_dpfm)
        bdpf.wdone(v_dpfm)

        bwmbv.rd("tensor")
        bln.rd("tensor")
        wait("tensor", v_brow)
        wait("tensor", v_ones)
        for b in range(NB):
            bbuf[2].wr("tensor")
            bbuf[3].wr("tensor")
            for i in range(NI):
                op("tensor", lambda t, i=i, b=b: t.matmul(
                    banks[2][:, :], lnXt[:, i * S + b * 128:i * S + (b + 1) * 128],
                    wmbv_g[:, i * 640:i * 640 + 512], start=(i == 0), stop=False))
                op("tensor", lambda t, i=i, b=b: t.matmul(
                    banks[3][:, 0:128], lnXt[:, i * S + b * 128:i * S + (b + 1) * 128],
                    wmbv_g[:, i * 640 + 512:(i + 1) * 640], start=(i == 0), stop=False))
            op("tensor", lambda t, g=g: t.matmul(
                banks[2][:, :], ones_row[0:1, :],
                brow[0:1, 512 + g * 512:512 + (g + 1) * 512], start=False, stop=True))
            v_mbvmm = op("tensor", lambda t, g=g: t.matmul(
                banks[3][:, 0:128], ones_row[0:1, :],
                brow[0:1, g * 128:(g + 1) * 128], start=False, stop=True), inc=True)
            bbuf[2].wdone(v_mbvmm)
            bbuf[3].wdone(v_mbvmm)
            bmb.wr("scalar")
            bv_.wr("scalar")
            bbuf[2].rd("scalar")
            bbuf[3].rd("scalar")
            op("scalar", lambda s, b=b: s.activation(
                mb_g[:, b * 512:(b + 1) * 512], banks[2][:, :], AF.Sigmoid))
            v_vev = op("scalar", lambda s, b=b: s.activation(
                v_g[:, b * 128:(b + 1) * 128], banks[3][:, 0:128], AF.Identity),
                inc=True)
            bbuf[2].rdone(v_vev)
            bbuf[3].rdone(v_vev)
            if b == NB - 1:
                bmb.wdone(v_vev)
                bv_.wdone(v_vev)
        bwmbv.rdone(("p_t", B.sems["p_t"][1]))

        tbk = banks[4].ap().bitcast(BF16)
        bxck.rd("tensor")
        bxctm.wr("vector")
        for half in range(2):
            bbuf[4].wr("tensor")
            for b in range(NB):
                op("tensor", lambda t, b=b, half=half, tbk=tbk: t.transpose(
                    tbk[:, b * 128:(b + 1) * 128],
                    xcat_k[:, half * S + b * 128:half * S + (b + 1) * 128],
                    ident[:, :]))
            v_ktr = mark("tensor")
            bbuf[4].wdone(v_ktr)
            bbuf[4].rd("vector")
            v_xc1 = op("vector", lambda v, half=half, tbk=tbk: v.tensor_copy(
                AP(xcat_tm, 3 + half * 128, [[NB * 259, 128], [259, NB], [1, 128]]),
                tbk[:, 0:NB * 128].rearrange("p (b n) -> p b n", b=NB)), inc=True)
            bbuf[4].rdone(v_xc1)
        bxck.rdone(v_xc1)
        bxctm.wdone(v_xc1)
        bxctm.wr("vector")
        vwrap = op("vector", lambda v: v.tensor_copy(
            AP(xcat_tm, 0, [[NB * 259, 128], [259, NB], [1, 3]]),
            AP(xcat_tm, 256, [[NB * 259, 128], [259, NB], [1, 3]])), inc=True)
        B.waitf("vector", vwrap)
        bdpt.wr("vector")
        for b in range(NB):
            op("vector", lambda v, b=b: v.tensor_tensor(
                AP(dpk_tm, b * F, [[NB * F, 128], [256, 3], [1, 256]]),
                AP(xcat_tm, b * 259 + 3, [[NB * 259, 128], [0, 3], [1, 256]]),
                AP(xcat_tm, b * 259 + 2, [[NB * 259, 128], [-1, 3], [1, 256]]),
                ALU.mult))
        v_dptm = mark("vector")
        bxctm.wdone(v_dptm)
        bdpt.wdone(v_dptm)

        bdpt.rd("scalar")
        blnsb[0].wr("scalar")
        bsca.wr("scalar")
        for b in range(NB):
            ev = op("scalar", lambda s, b=b: s.activation(
                ln_sb[:, 0:F], dpk_tm[:, b * F:(b + 1) * F], AF.Square,
                accum_out=sca[:, C_SSQ + b:C_SSQ + b + 1]), inc=True)
            B.waitf("scalar", ev)
        v_ssq = ev
        blnsb[0].wdone(v_ssq)
        bdpt.rdone(v_ssq)

        wait("vector", v_ssq)
        B.waitf("scalar", v_ssq)
        op("scalar", lambda s: s.activation(sca[:, C_N:C_N + 8],
                                            sca[:, C_SSQ:C_SSQ + 8], AF.Sqrt))
        v_sc1 = op("scalar", lambda s: s.activation(
            sca[:, C_EN:C_EN + 8], sca[:, C_SSQ:C_SSQ + 8], AF.Sqrt,
            scale=cst[:, 1:2]), inc=True)
        wait("vector", v_sc1)
        B.sop("vector", lambda v: v.tensor_scalar_max(sca[:, C_N:C_N + 8],
                                                      sca[:, C_N:C_N + 8], 1e-12))
        B.sop("vector", lambda v: v.tensor_scalar_max(sca[:, C_EN:C_EN + 8],
                                                      sca[:, C_EN:C_EN + 8], 1e-20))
        B.sop("vector", lambda v: v.reciprocal(sca[:, C_INVN:C_INVN + 8],
                                               sca[:, C_N:C_N + 8]))
        B.sop("vector", lambda v: v.reciprocal(sca[:, C_RSSQ:C_RSSQ + 8],
                                               sca[:, C_SSQ:C_SSQ + 8]))
        bvn.wr("vector")
        for b in range(NB):
            B.sop("vector", lambda v, b=b: v.tensor_scalar_mul(
                v_g[:, b * 128:(b + 1) * 128], v_g[:, b * 128:(b + 1) * 128],
                sca[:, C_INVN + b:C_INVN + b + 1]))
        v_vn = mark("vector")
        bvn.wdone(v_vn)

        bdpf.rd("tensor")
        bmn.rd("tensor")
        bpv.wr("vector")
        for b in range(NB):
            bbuf[(b % 2) * 2].wr("tensor")
            bbuf[(b % 2) * 2 + 1].wr("tensor")
            for c in range(NF):
                for pair in range(2):
                    op("tensor", lambda t, b=b, c=c, pair=pair, nb0=(b % 2) * 2:
                       t.matmul(
                        banks[nb0 + pair][:, 0:258],
                        dpk_fm[:, c * S + b * 128:c * S + (b + 1) * 128],
                        mn_g[:, (pair * NF + c) * 258:(pair * NF + c + 1) * 258],
                        start=(c == 0), stop=(c == NF - 1)))
            nb0 = (b % 2) * 2
            v_nmm = mark("tensor")
            bbuf[nb0].wdone(v_nmm)
            bbuf[nb0 + 1].wdone(v_nmm)
            bbuf[nb0].rd("vector")
            bbuf[nb0 + 1].rd("vector")
            op("vector", lambda v, b=b, nb0=nb0: v.tensor_copy(
                pv_raw[:, b * 516:b * 516 + 258], banks[nb0][:, 0:258]))
            v_pvev = op("vector", lambda v, b=b, nb0=nb0: v.tensor_copy(
                pv_raw[:, b * 516 + 258:(b + 1) * 516], banks[nb0 + 1][:, 0:258]),
                inc=True)
            bbuf[nb0].rdone(v_pvev)
            bbuf[nb0 + 1].rdone(v_pvev)
        bpv.wdone(v_pvev)
        v_numt_last = ("p_t", B.sems["p_t"][1])
        bdpf.rdone(v_numt_last)
        bmn.rdone(v_numt_last)

        bpv.rd("vector")
        B.sop("vector", lambda v: v.tensor_copy(
            sca[:, C_RK:C_RK + 32],
            AP(pv_raw, 128, [[NB * 516, 128], [516, NB], [129, 4]])))
        B.sop("vector", lambda v: v.tensor_tensor(
            sca[:, C_DPV:C_DPV + 32], sca[:, C_RK:C_RK + 32],
            AP(sca, C_EN, [[288, 128], [1, 8], [0, 4]]), ALU.add))
        B.sop("vector", lambda v: v.tensor_tensor(
            sca[:, C_TMUL:C_TMUL + 32],
            AP(sca, C_N, [[288, 128], [1, 8], [0, 4]]),
            sca[:, C_DPV:C_DPV + 32], ALU.mult))
        B.sop("vector", lambda v: v.reciprocal(sca[:, C_NC1:C_NC1 + 32],
                                               sca[:, C_TMUL:C_TMUL + 32]))
        B.sop("vector", lambda v: v.tensor_scalar_mul(sca[:, C_NC1:C_NC1 + 32],
                                                      sca[:, C_NC1:C_NC1 + 32], -1.0))
        v_ch1 = B.sop("vector", lambda v: v.tensor_tensor(
            sca[:, C_COEF:C_COEF + 32], sca[:, C_TMUL:C_TMUL + 32],
            AP(sca, C_RSSQ, [[288, 128], [1, 8], [0, 4]]), ALU.mult))
        wait("scalar", v_ch1)
        B.waitf("scalar", v_sc1)
        v_ch2 = op("scalar", lambda s: s.activation(
            sca[:, C_COEF:C_COEF + 32], sca[:, C_COEF:C_COEF + 32], AF.Relu,
            scale=-1.0, bias=1.0), inc=True)
        wait("vector", v_ch2)
        B.sop("vector", lambda v: v.tensor_scalar_min(sca[:, C_COEF:C_COEF + 32],
                                                      sca[:, C_COEF:C_COEF + 32], 1.0))
        bcoef.wr("vector")
        v_chain = B.sop("vector", lambda v: v.tensor_tensor(
            coefb[:, :], sca[:, C_COEF:C_COEF + 32],
            AP(sca, C_INVN, [[288, 128], [1, 8], [0, 4]]), ALU.mult))
        bcoef.wdone(v_chain)
        bsca.wdone(v_chain)

        for j in range(4):
            h = 4 * g + j
            bwmv.wr("vector")
            bpv.rd("vector")
            bvn.rd("vector")
            bmb.rd("vector")
            for b in range(NB):
                B.sop("vector", lambda v, b=b, j=j: v.scalar_tensor_tensor(
                    wmv_h[:, b * 128:(b + 1) * 128],
                    pv_raw[:, b * 516 + j * 129:b * 516 + j * 129 + 128],
                    sca[:, C_NC1 + 4 * b + j:C_NC1 + 4 * b + j + 1],
                    v_g[:, b * 128:(b + 1) * 128], ALU.mult, ALU.add))
                B.sop("vector", lambda v, b=b, j=j: v.tensor_mul(
                    wmv_h[:, b * 128:(b + 1) * 128],
                    wmv_h[:, b * 128:(b + 1) * 128],
                    mb_g[:, b * 512 + j * 128:b * 512 + (j + 1) * 128]))
            v_wmv = mark("vector")
            bwmv.wdone(v_wmv)
            bpv.rdone(v_wmv)
            bvn.rdone(v_wmv)
            bmb.rdone(v_wmv)
            bwmv.rd("tensor")
            bdpt.rd("tensor")
            bbuf[6].wr("tensor")
            bbuf[7].wr("tensor")
            for b in range(NB):
                op("tensor", lambda t, b=b: t.matmul(
                    banks[6][:, :], wmv_h[:, b * 128:(b + 1) * 128],
                    dpk_tm[:, b * F:b * F + 512], start=(b == 0), stop=(b == NB - 1)))
                op("tensor", lambda t, b=b: t.matmul(
                    banks[7][:, 0:256], wmv_h[:, b * 128:(b + 1) * 128],
                    dpk_tm[:, b * F + 512:(b + 1) * F], start=(b == 0),
                    stop=(b == NB - 1)))
            v_dmm = mark("tensor")
            bwmv.rdone(v_dmm)
            bdpt.rdone(v_dmm)
            bbuf[6].wdone(v_dmm)
            bbuf[7].wdone(v_dmm)
            bds.wr("scalar")
            bbuf[6].rd("scalar")
            bbuf[7].rd("scalar")
            op("scalar", lambda s: s.activation(dstage[:, 0:512], banks[6][:, :],
                                                AF.Identity), inc=True)
            v_dev = op("scalar", lambda s: s.activation(
                dstage[:, 512:F], banks[7][:, 0:256], AF.Identity), inc=True)
            bbuf[6].rdone(v_dev)
            bbuf[7].rdone(v_dev)
            bds.wdone(v_dev)
            bds.rd("tensor")
            wait("tensor", v_id)
            bbuf[6].wr("tensor")
            bbuf[7].wr("tensor")
            for ct in range(6):
                bk_, cc = (banks[6], ct) if ct < 4 else (banks[7], ct - 4)
                op("tensor", lambda t, ct=ct, bk_=bk_, cc=cc: t.transpose(
                    bk_.bitcast(BF16)[:, cc * 128:(cc + 1) * 128],
                    dstage[:, ct * 128:(ct + 1) * 128], ident[:, :]))
            v_dtr = mark("tensor")
            bds.rdone(v_dtr)
            bbuf[6].wdone(v_dtr)
            bbuf[7].wdone(v_dtr)
            bds2.wr("scalar")
            bbuf[6].rd("scalar")
            bbuf[7].rd("scalar")
            op("scalar", lambda s: s.activation(
                dstage2[:, 0:512], banks[6].bitcast(BF16)[:, 0:512],
                AF.Identity), inc=True)
            v_dev2 = op("scalar", lambda s: s.activation(
                dstage2[:, 512:F], banks[7].bitcast(BF16)[:, 0:256],
                AF.Identity), inc=True)
            bbuf[6].rdone(v_dev2)
            bbuf[7].rdone(v_dev2)
            bds2.wdone(v_dev2)
            bds2.rd("sync")
            v_dd = dma("sync",
                       AP(ard_in, h * 128,
                          [[NH * 128, 128], [128 * NH * 128, 6], [1, 128]]),
                       dstage2[:, :].rearrange("p (c n) -> p c n", c=6))
            bds2.rdone(v_dd)

        bcoef.rd("tensor")
        bdpt.rd("tensor")
        bbuf[5].wr("tensor")
        bbuf[4].wr("tensor")
        for b in range(NB):
            op("tensor", lambda t, b=b: t.matmul(
                banks[5][0:4, :], coefb[:, 4 * b:4 * (b + 1)],
                dpk_tm[:, b * F:b * F + 512], start=(b == 0), stop=(b == NB - 1)))
            op("tensor", lambda t, b=b: t.matmul(
                banks[4][0:4, 0:256], coefb[:, 4 * b:4 * (b + 1)],
                dpk_tm[:, b * F + 512:(b + 1) * F], start=(b == 0),
                stop=(b == NB - 1)))
        v_dnmm = mark("tensor")
        bcoef.rdone(v_dnmm)
        bdpt.rdone(v_dnmm)
        bbuf[5].wdone(v_dnmm)
        bbuf[4].wdone(v_dnmm)
        bdn.wr("scalar")
        bbuf[5].rd("scalar")
        bbuf[4].rd("scalar")
        op("scalar", lambda s: s.activation(dstage[0:4, 0:512], banks[5][0:4, :],
                                            AF.Identity), inc=True)
        v_dnev = op("scalar", lambda s: s.activation(
            dstage[0:4, 512:F], banks[4][0:4, 0:256], AF.Identity), inc=True)
        bbuf[5].rdone(v_dnev)
        bbuf[4].rdone(v_dnev)
        bdn.wdone(v_dnev)
        bdn.rd("sync")
        v_dnd = dma("sync", AP(arn_in, 4 * g * F, [[F, 4], [1, F]]),
                    dstage[0:4, :])
        bdn.rdone(v_dnd)

    B.wait_all_dma("gpsimd", "sync")
    _arh = 384 * NH * 128
    v_ar1 = B.op_cc("gpsimd", lambda g2: g2.collective_compute(
        "AllReduce", ALU.add, replica_groups=[list(range(NCORES))],
        ins=[AP(ard_in, 0, [[1, _arh]])], outs=[AP(ard_out, 0, [[1, _arh]])]))
    v_arn = B.op_cc("gpsimd", lambda g2: g2.collective_compute(
        "AllReduce", ALU.add, replica_groups=[list(range(NCORES))],
        ins=[arn_in.ap()], outs=[arn_out.ap()]))
    v_ar = B.op_cc("gpsimd", lambda g2: g2.collective_compute(
        "AllReduce", ALU.add, replica_groups=[list(range(NCORES))],
        ins=[AP(ard_in, _arh, [[1, _arh]])],
        outs=[AP(ard_out, _arh, [[1, _arh]])]))

    B.join()
    stA.close()
    stQ.close()

    stB = ExitStack()
    eB = stB.enter_context
    newmm = eB(nc.sbuf_tensor("newmm", [128, NF * NH * 129], BF16))
    memh = eB(nc.sbuf_tensor("memh", [128, NB * DM], BF16))
    bnew = B.buf(newmm, "newmm")
    bmemh = B.buf(memh, "memh")

    bnew.wr("gpsimd")
    v_mnld = dma("gpsimd", newmm[:, :],
                 AP(mn_seq.tensor, 0, [[NF * 2064, 128], [1, NF * 2064]]))
    bnew.wdone(v_mnld)

    stB2 = ExitStack()
    eB2 = stB2.enter_context
    wq_h = eB2(nc.sbuf_tensor("wq_h", [128, 2 * NI * 128], BF16))
    xcat_q = eB2(nc.sbuf_tensor("xcat_q", [128, 2 * S], BF16))
    dpq = eB2(nc.sbuf_tensor("dpq", [128, 2 * NF * S], BF16))
    sqslots = eB2(nc.sbuf_tensor("sqslots", [128, 2 * S], BF16))
    rowbuf = ln_sb[0:1, :].bitcast(F32)
    bwq = [B.buf(wq_h, "wq_h0"), B.buf(wq_h, "wq_h1")]
    bxcq_ = B.buf(xcat_q, "xcat_q")
    bxcq = [bxcq_, bxcq_]
    bdpq = [B.buf(dpq, "dpq0"), B.buf(dpq, "dpq1")]
    bsqs = [B.buf(sqslots, "sq_s0"), B.buf(sqslots, "sq_s1")]
    brown = blnsb[0]

    pend = {}

    def q_prep(h):
        par = h % 2
        xq = 0
        sq = 0
        dq = par * NF * S
        wq0 = par * NI * 128
        bwq[par].wr("gpsimd")
        v_wq = dma("gpsimd", wq_h[:, wq0:wq0 + NI * 128],
                   AP(wqT.tensor, h * 128 * NI * 128,
                      [[NI * 128, 128], [1, NI * 128]]))
        bwq[par].wdone(v_wq)
        bwq[par].rd("tensor")
        bln.rd("tensor")
        bbuf[0].wr("tensor")
        bbuf[1].wr("tensor")
        for i in range(NI):
            for h2 in range(2):
                op("tensor", lambda t, i=i, h2=h2, wq0=wq0: t.matmul(
                    banks[h2][:, :],
                    wq_h[:, wq0 + i * 128:wq0 + (i + 1) * 128],
                    lnXt[:, i * S + h2 * 512:i * S + (h2 + 1) * 512],
                    start=(i == 0), stop=(i == NI - 1)))
        v_qmm = mark("tensor")
        bwq[par].rdone(v_qmm)
        bbuf[0].wdone(v_qmm)
        bbuf[1].wdone(v_qmm)
        bxcq[par].wr("scalar")
        bbuf[0].rd("scalar")
        bbuf[1].rd("scalar")
        for h2 in range(2):
            op("scalar", lambda s, h2=h2, h=h, xq=xq: s.activation(
                xcat_q[:, xq + h2 * 512:xq + (h2 + 1) * 512], banks[h2][:, :],
                AF.Relu, bias=bkq[:, NKV + h:NKV + h + 1]))
            op("scalar", lambda s, h2=h2, h=h, xq=xq: s.activation(
                xcat_q[:, xq + S + h2 * 512:xq + S + (h2 + 1) * 512],
                banks[h2][:, :], AF.Relu,
                bias=bkq[:, 2 * NKV + NH + h:2 * NKV + NH + h + 1], scale=-1.0))
        v_qevac = mark("scalar")
        bxcq[par].wdone(v_qevac)
        bbuf[0].rdone(v_qevac)
        bbuf[1].rdone(v_qevac)
        bdpq[par].wr("vector")
        wait("tensor", v_roll)
        v_dpq = None
        for j in (1, 2, 3):
            bxcq[par].rd("tensor")
            for half in range(2):
                bks = (2, 3) if half == 0 else (0, 1)
                for c2 in range(2):
                    bbuf[bks[c2]].wr("tensor")
                    op("tensor", lambda t, j=j, half=half, c2=c2, bks=bks: t.matmul(
                        banks[bks[c2]][:, :],
                        rollm[:, (j - 1) * 128:j * 128],
                        xcat_q[:, half * S + c2 * 512:half * S + (c2 + 1) * 512],
                        start=True, stop=False))
                    op("tensor", lambda t, j=j, half=half, c2=c2, bks=bks: t.matmul(
                        banks[bks[c2]][:, :],
                        rollm[:, (2 + j) * 128:(3 + j) * 128],
                        xcat_q[:, (1 - half) * S + c2 * 512:
                               (1 - half) * S + (c2 + 1) * 512],
                        start=False, stop=True))
            v_shmm = mark("tensor")
            for bk in (0, 1, 2, 3):
                bbuf[bk].wdone(v_shmm)
            bxcq[par].rdone(v_shmm)
            wait("vector", v_shmm)
            bxcq[par].rd("vector")
            for half in range(2):
                bks = (2, 3) if half == 0 else (0, 1)
                t6 = (j - 1) * 2 + half
                for c2 in range(2):
                    bbuf[bks[c2]].rd("vector")
                    v_dpq = op("vector", lambda v, t6=t6, half=half, dq=dq,
                               c2=c2, bks=bks: v.tensor_mul(
                        dpq[:, dq + t6 * S + c2 * 512:dq + t6 * S + (c2 + 1) * 512],
                        xcat_q[:, half * S + c2 * 512:half * S + (c2 + 1) * 512],
                        banks[bks[c2]][:, :]), inc=True)
            for bk in (0, 1, 2, 3):
                bbuf[bk].rdone(v_dpq)
            bxcq[par].rdone(v_dpq)
        bdpq[par].wdone(v_dpq)
        pend[("dpq", h)] = v_dpq

    def q_prep_b(h):
        par = h % 2
        dq = par * NF * S
        wait("scalar", pend[("dpq", h)])
        wait("tensor", v_ones)
        bbuf[5].wr("tensor")
        bbuf[6].wr("tensor")
        v_mm = None
        for c in range(NF):
            sl = c % 2
            slot = sqslots[:, sl * S:(sl + 1) * S]
            bsqs[sl].wr("scalar")
            ev = op("scalar", lambda s, c=c, dq=dq, slot=slot: s.activation(
                slot, dpq[:, dq + c * S:dq + (c + 1) * S], AF.Square), inc=True)
            bsqs[sl].wdone(ev)
            wait("tensor", ev)
            bsqs[sl].rd("tensor")
            for h2 in range(2):
                op("tensor", lambda t, c=c, h2=h2, slot=slot: t.matmul(
                    banks[5 + h2][0:1, :], ones_col[:, 0:1],
                    slot[:, h2 * 512:(h2 + 1) * 512],
                    start=(c == 0), stop=(c == NF - 1)))
            v_mm = mark("tensor")
            bsqs[sl].rdone(v_mm)
        bbuf[5].wdone(v_mm)
        bbuf[6].wdone(v_mm)
        pend[("rows", h)] = v_mm

    def q_fin(h):
        par = h % 2
        brown.wr("scalar")
        bbuf[5].rd("scalar")
        bbuf[6].rd("scalar")
        wait("scalar", pend[("rows", h)])
        op("scalar", lambda s: s.activation(rowbuf[0:1, 0:512], banks[5][0:1, :],
                                            AF.Identity), inc=True)
        v_row = op("scalar", lambda s: s.activation(
            rowbuf[0:1, 512:1024], banks[6][0:1, :], AF.Identity), inc=True)
        bbuf[5].rdone(v_row)
        bbuf[6].rdone(v_row)
        brown.wdone(v_row)
        brown.rd("tensor")
        wait("tensor", v_row)
        bbuf[7].wr("tensor")
        for b in range(NB):
            op("tensor", lambda t, b=b: t.matmul(
                banks[7][:, b:b + 1], rowbuf[0:1, b * 128:(b + 1) * 128],
                ones_f32[0:1, 0:1], start=True, stop=True))
        v_mini = mark("tensor")
        brown.rdone(v_mini)
        bbuf[7].wdone(v_mini)
        wait("scalar", v_mini)
        B.waitf("scalar", pend.get(("qen_s", par)))
        wait("scalar", pend.get(("qen_rd", par)))
        bbuf[7].rd("scalar")
        v_qssq = op("scalar", lambda s, par=par: s.activation(
            sca[:, C_QSSQ + par * 8:C_QSSQ + par * 8 + 8], banks[7][:, 0:8],
            AF.Identity), inc=True)
        bbuf[7].rdone(v_qssq)
        B.waitf("scalar", v_qssq)
        v_qen = op("scalar", lambda s, par=par: s.activation(
            sca[:, C_QEN + par * 8:C_QEN + par * 8 + 8],
            sca[:, C_QSSQ + par * 8:C_QSSQ + par * 8 + 8], AF.Sqrt,
            scale=cst[:, 1:2]), inc=True)
        pend[("qen_s", par)] = v_qen
        wait("vector", v_qen)
        v_qen2 = B.sop("vector", lambda v, par=par: v.tensor_scalar_max(
            sca[:, C_QEN + par * 8:C_QEN + par * 8 + 8],
            sca[:, C_QEN + par * 8:C_QEN + par * 8 + 8], 1e-20))
        pend[("qen", h)] = v_qen2

    def num_a(h):
        par = h % 2
        dq = par * NF * S
        bnew.rd("tensor")
        bdpq[par].rd("tensor")
        groups = [(2, [0, 1, 2]), (3, [3, 4, 5]), (4, [6, 7])]
        evs = []
        for bank_i, bs in groups:
            bk_ = banks[bank_i]
            bbuf[bank_i].wr("tensor")
            for k, b in enumerate(bs):
                for c in range(NF):
                    op("tensor", lambda t, b=b, k=k, c=c, h=h, bk_=bk_, dq=dq: t.matmul(
                        bk_[:, k * 129:(k + 1) * 129],
                        dpq[:, dq + c * S + b * 128:dq + c * S + (b + 1) * 128],
                        newmm[:, c * NH * 129 + h * 129:c * NH * 129 + (h + 1) * 129],
                        start=(c == 0), stop=(c == NF - 1)))
            v_na = mark("tensor")
            bbuf[bank_i].wdone(v_na)
        cw = C_NAW + par * 8
        ce = C_QEN + par * 8
        wait("scalar", pend[("qen", h)])
        B.waitf("scalar", pend.get(("naw_s", par)))
        ev_w = None
        for gi_, (bank_i, bs) in enumerate(groups):
            bbuf[bank_i].rd("scalar")
            ev_w = op("scalar", lambda s, bank_i=bank_i, bs=bs, cw=cw: s.activation(
                sca[:, cw + bs[0]:cw + bs[0] + len(bs)],
                AP(banks[bank_i], 128, [[512, 128], [129, len(bs)]]),
                AF.Identity), inc=True)
        wait("vector", ev_w)
        B.sop("vector", lambda v, cw=cw, ce=ce: v.tensor_add(
            sca[:, cw:cw + 8], sca[:, cw:cw + 8], sca[:, ce:ce + 8]))
        v_inv = B.sop("vector", lambda v, cw=cw: v.reciprocal(sca[:, cw:cw + 8],
                                                              sca[:, cw:cw + 8]))
        wait("scalar", v_inv)
        v_mh = None
        for bank_i, bs in groups:
            for k, b in enumerate(bs):
                v_mh = op("scalar", lambda s, b=b, k=k, h=h, bank_i=bank_i, cw=cw:
                          s.activation(
                              memh[:, b * DM + h * 128:b * DM + (h + 1) * 128],
                              banks[bank_i][:, k * 129:k * 129 + 128],
                              AF.Identity, scale=sca[:, cw + b:cw + b + 1]),
                          inc=True)
            bbuf[bank_i].rdone(v_mh)
        bdpq[par].rdone(("p_t", B.sems["p_t"][1]))
        pend[("qen_rd", par)] = v_mh
        pend[("naw_s", par)] = v_mh
        if h == 0:
            bmemh.wr("vector")
        if h == NH - 1:
            bmemh.wdone(v_mh)

    q_prep(0)
    q_prep_b(0)
    q_fin(0)
    q_prep(1)
    q_prep_b(1)
    q_fin(1)

    stB1 = ExitStack()
    dadd = stB1.enter_context(nc.sbuf_tensor("dadd", [128, 2 * 1024], BF16))
    normrows = stB1.enter_context(nc.sbuf_tensor("normrows", [16, F], BF16))
    bdadd = [B.buf(dadd, "dadd0"), B.buf(dadd, "dadd1")]
    bnr = B.buf(normrows, "normrows")
    wait("sync", v_arn)
    bnr.wr("sync")
    v_nr = dma("sync", normrows[0:16, :], AP(arn_out, 0, [[F, 16], [1, F]]))
    bnr.wdone(v_nr)
    bnew.wr("vector")
    for c in range(NF):
        for hf in range(2):
            k = 2 * c + hf
            sl = k % 2
            qn = ("sync", "scalar")[k % 2]
            wait(qn, v_ar1 if c < 3 else v_ar)
            bdadd[sl].wr(qn)
            v_d1 = dma(qn, dadd[:, sl * 1024:(sl + 1) * 1024],
                       AP(ard_out, c * 128 * NH * 128 + hf * 1024,
                          [[NH * 128, 128], [1, 1024]]))
            bdadd[sl].wdone(v_d1)
            wait("vector", v_d1)
            bdadd[sl].rd("vector")
            v_na = op("vector", lambda v, c=c, hf=hf, sl=sl: v.tensor_tensor(
                AP(newmm, c * NH * 129 + hf * 8 * 129,
                   [[NF * NH * 129, 128], [129, 8], [1, 128]]),
                AP(newmm, c * NH * 129 + hf * 8 * 129,
                   [[NF * NH * 129, 128], [129, 8], [1, 128]]),
                AP(dadd, sl * 1024, [[2 * 1024, 128], [128, 8], [1, 128]]),
                ALU.add), inc=True)
            bdadd[sl].rdone(v_na)
        bnr.rd("tensor")
        wait("tensor", v_id)
        bbuf[2].wr("tensor")
        wait("tensor", v_nr)
        op("tensor", lambda t, c=c: t.transpose(
            banks[2].bitcast(BF16)[:, 0:NH], normrows[0:16, c * 128:(c + 1) * 128],
            ident[0:16, 0:16]))
        v_tr = mark("tensor")
        bnr.rdone(v_tr)
        bbuf[2].wdone(v_tr)
        bbuf[2].rd("vector")
        v_nadd = op("vector", lambda v, c=c: v.tensor_add(
            AP(newmm, c * NH * 129 + 128, [[NF * NH * 129, 128], [129, NH]]),
            AP(newmm, c * NH * 129 + 128, [[NF * NH * 129, 128], [129, NH]]),
            banks[2].bitcast(BF16)[:, 0:NH]), inc=True)
        bbuf[2].rdone(v_nadd)
    bnew.wdone(v_nadd)
    wait("gpsimd", v_nadd)
    v_ag = emit_ag()
    stB1.close()

    for h in range(NH):
        num_a(h)
        if h + 2 < NH:
            q_prep(h + 2)
        if h + 1 < NH and h >= 1:
            q_prep_b(h + 1)
            q_fin(h + 1)

    if DEBUG_TAPS:
        bmemh.rd("sync")
        dma("sync", dbg_memh, memh[:, :])
        bnew.rd("sync")
        dma("sync", dbg_nm, newmm[:, :])

    fenceB2 = B.fence_evs()
    stB2.close()
    stB3 = ExitStack()
    eB3 = stB3.enter_context
    wg_f = eB3(nc.sbuf_tensor("wg_f", [128, NI * 512], BF16))
    sgf32 = eB3(nc.sbuf_tensor("sgf32", [128, 512], F32))
    tscr = eB3(nc.sbuf_tensor("tscr", [128, 512], F32))
    bgrow = eB3(nc.sbuf_tensor("bgrow", [1, DM], BF16))
    bwg = B.buf(wg_f, "wg_f")
    bsg = B.buf(sgf32, "sgf32")
    bts = B.buf(tscr, "tscr")
    bbg = B.buf(bgrow, "bgrow")
    for _b in (bwg, bsg, bts, bbg):
        _b.seed(fenceB2)
    bbg.wr("gpsimd")
    v_bgrow = dma("gpsimd", bgrow[0:1, :], bg_h.unsqueeze(0))
    bbg.wdone(v_bgrow)
    _ln2st = [None]
    for og in range(4):
        bwg.wr("gpsimd")
        v_wg = dma("gpsimd", wg_f[:, :],
                   AP(wgT.tensor, og * 128 * NI * 512,
                      [[NI * 512, 128], [1, NI * 512]]))
        bwg.wdone(v_wg)
        bwg.rd("tensor")
        bln.rd("tensor")
        wait("tensor", v_bgrow)
        for b in range(NB):
            bbuf[0].wr("tensor")
            for i in range(NI):
                op("tensor", lambda t, i=i, b=b: t.matmul(
                    banks[0][:, :], lnXt[:, i * S + b * 128:i * S + (b + 1) * 128],
                    wg_f[:, i * 512:(i + 1) * 512], start=(i == 0), stop=False))
            v_gmm = op("tensor", lambda t, og=og: t.matmul(
                banks[0][:, :], ones_row[0:1, :],
                bgrow[0:1, og * 512:(og + 1) * 512], start=False, stop=True),
                inc=True)
            bbuf[0].wdone(v_gmm)
            bsg.wr("scalar")
            bbuf[0].rd("scalar")
            v_sg = op("scalar", lambda s: s.activation(
                sgf32[:, :], banks[0][:, :], AF.Sigmoid), inc=True)
            bbuf[0].rdone(v_sg)
            bsg.wdone(v_sg)
            bsg.rd("vector")
            bmemh.rd("vector")
            bts.wr("vector")
            B.sop("vector", lambda v, b=b, og=og: v.tensor_mul(
                tscr[:, :], sgf32[:, :],
                memh[:, b * DM + og * 512:b * DM + (og + 1) * 512]))
            bx.wr("vector")
            v_ao = B.sop("vector", lambda v, b=b, og=og: v.tensor_add(
                x_tm[:, b * DM + og * 512:b * DM + (og + 1) * 512],
                x_tm[:, b * DM + og * 512:b * DM + (og + 1) * 512],
                tscr[:, :]))
            bsg.rdone(v_ao)
            bmemh.rdone(v_ao)
            bts.wdone(v_ao)
            bx.wdone(v_ao)
            if og == 3:
                bx.rd("scalar")
                bmemh.wr("scalar")
                wait("scalar", v_ao)
                ev3 = op("scalar", lambda s, b=b: s.activation(
                    memh[:, b * DM:(b + 1) * DM], x_tm[:, b * DM:(b + 1) * DM],
                    AF.Identity,
                    accum_out=sca[:, C_LM + b:C_LM + b + 1]), inc=True)
                B.waitf("scalar", ev3)
                _ln2st[0] = op("scalar", lambda s, b=b: s.activation(
                    memh[:, b * DM:(b + 1) * DM], x_tm[:, b * DM:(b + 1) * DM],
                    AF.Square,
                    accum_out=sca[:, C_LI + b:C_LI + b + 1]), inc=True)
                bmemh.wdone(_ln2st[0])
        bwg.rdone(("p_t", B.sems["p_t"][1]))

    if DEBUG_TAPS:
        bx.rd("sync")
        dma("sync", dbg_x2, x_tm[:, :])
    B.join()
    stB3.close()
    stB.close()

    emit_layernorm(update_xa=False, v_stats_pre=_ln2st[0])

    stC = ExitStack()
    eC = stC.enter_context
    gsus = eC(nc.sbuf_tensor("gsus", [128, 2 * NIC], F32))
    dma("sync", gsus[:, 0:NIC], AP(gs_h.tensor, 0, [[1, 128], [128, NIC]]),
        allow_slow_non_contiguous=True)
    v_gsus = dma("sync", gsus[:, NIC:2 * NIC],
                 AP(us_h.tensor, 0, [[1, 128], [128, NIC]]),
                 cont=True, allow_slow_non_contiguous=True)
    fuse = eC(nc.sbuf_tensor("fuse", [128, NIC * 512], BF16))
    bfuse = B.buf(fuse, "fuse")

    wait("gpsimd", v_ag)
    wait("sync", v_ag)

    def ag_gu_w(base_off, w, c0, n):
        return AP(ag_out, base_off + w * 128 * I_FFN + c0,
                  [[I_FFN, 128], [SEC, 8], [1, n]])

    def ag_down_w(w, c0, n):
        return AP(ag_out, 2 * 256 * I_FFN + w * 128 * DM + c0,
                  [[DM, 128], [SEC, 8], [1, n]])

    gslab = eC(nc.sbuf_tensor("gslab", [128, 2 * NI * 256], F8))
    uslab = eC(nc.sbuf_tensor("uslab", [128, 2 * NI * 256], F8))
    sgf = eC(nc.sbuf_tensor("sgf", [128, 512], BF16))
    upf = eC(nc.sbuf_tensor("upf", [128, 512], BF16))
    dslab = eC(nc.sbuf_tensor("dslab", [128, 4 * 12 * 256], F8))
    oacc = eC(nc.sbuf_tensor("oacc", [128, 256], F32))
    ds_sl = eC(nc.sbuf_tensor("ds_sl", [128, 2 * 256], F32))
    bgsl2 = [B.buf(gslab, "gslab0"), B.buf(gslab, "gslab1")]
    busl2 = [B.buf(uslab, "uslab0"), B.buf(uslab, "uslab1")]
    bsgf = B.buf(sgf, "sgf")
    bupf = B.buf(upf, "upf")
    bdsl2 = [B.buf(dslab, "dslab0"), B.buf(dslab, "dslab1"),
             B.buf(dslab, "dslab2"), B.buf(dslab, "dslab3")]
    boa = B.buf(oacc, "oacc")
    bdss2 = [B.buf(ds_sl, "ds_sl0"), B.buf(ds_sl, "ds_sl1")]
    n_out = 0
    for sh in range(2):
        for og2 in range(24):
            sp = og2 % 2
            so = sp * NI * 256
            bgsl2[sp].wr("sync")
            for w in range(2):
                v_g1 = dma("sync",
                           AP(gslab, so + w * 256, [[2 * NI * 256, 128], [512, 8], [1, 256]]),
                           ag_gu_w(0, w, og2 * 256, 256), cont=(w > 0))
            busl2[sp].wr("sync")
            for w in range(2):
                v_sl = dma("sync",
                           AP(uslab, so + w * 256, [[2 * NI * 256, 128], [512, 8], [1, 256]]),
                           ag_gu_w(256 * I_FFN, w, og2 * 256, 256), cont=(w > 0))
            bgsl2[sp].wdone(v_g1)
            busl2[sp].wdone(v_sl)
            bgsl2[sp].rd("tensor")
            busl2[sp].rd("tensor")
            bln.rd("tensor")
            for oc in range(2):
                ic = og2 * 2 + oc
                fb0 = ((og2 * 2 + oc) % 2) * 2
                bbuf[fb0].wr("tensor")
                bbuf[fb0 + 1].wr("tensor")
                for i in range(NI):
                    op("tensor", lambda t, i=i, oc=oc, sh=sh, so=so, fb0=fb0:
                       t.matmul(
                        banks[fb0][:, :],
                        gslab[:, so + i * 256 + oc * 128:so + i * 256 + (oc + 1) * 128],
                        lnXt[:, i * S + sh * 512:i * S + (sh + 1) * 512],
                        start=(i == 0), stop=(i == NI - 1)))
                    op("tensor", lambda t, i=i, oc=oc, sh=sh, so=so, fb0=fb0:
                       t.matmul(
                        banks[fb0 + 1][:, :],
                        uslab[:, so + i * 256 + oc * 128:so + i * 256 + (oc + 1) * 128],
                        lnXt[:, i * S + sh * 512:i * S + (sh + 1) * 512],
                        start=(i == 0), stop=(i == NI - 1)))
                v_fmm = mark("tensor")
                bbuf[fb0].wdone(v_fmm)
                bbuf[fb0 + 1].wdone(v_fmm)
                wait("scalar", v_gsus)
                bsgf.wr("scalar")
                bupf.wr("scalar")
                bbuf[fb0].rd("scalar")
                bbuf[fb0 + 1].rd("scalar")
                op("scalar", lambda s, ic=ic, fb0=fb0: s.activation(
                    sgf[:, :], banks[fb0][:, :], AF.Silu, scale=gsus[:, ic:ic + 1]))
                v_upf = op("scalar", lambda s, ic=ic, fb0=fb0: s.activation(
                    upf[:, :], banks[fb0 + 1][:, :], AF.Identity,
                    scale=gsus[:, NIC + ic:NIC + ic + 1]), inc=True)
                bbuf[fb0].rdone(v_upf)
                bbuf[fb0 + 1].rdone(v_upf)
                bsgf.wdone(v_upf)
                bupf.wdone(v_upf)
                bfuse.wr("vector")
                bsgf.rd("vector")
                bupf.rd("vector")
                v_fu = op("vector", lambda v, ic=ic: v.tensor_mul(
                    fuse[:, ic * 512:(ic + 1) * 512], sgf[:, :], upf[:, :]),
                    inc=True)
                bsgf.rdone(v_fu)
                bupf.rdone(v_fu)
                if og2 == 23 and oc == 1:
                    bfuse.wdone(v_fu)
            bgsl2[sp].rdone(("p_t", B.sems["p_t"][1]))
            busl2[sp].rdone(("p_t", B.sems["p_t"][1]))

        for og in range(8):
            dp_ = og % 2
            bdss2[dp_].wr("sync")
            v_dss = dma("sync", ds_sl[:, dp_ * 256:(dp_ + 1) * 256],
                        AP(ds_h.tensor, og * 256, [[0, 128], [1, 256]]))
            bdss2[dp_].wdone(v_dss)
            for bb in range(4):
                bbuf[2 + bb].wr("tensor")
            bfuse.rd("tensor")
            v_pieces = []
            for pc in range(4):
                pp = (og * 4 + pc) % 4
                po = pp * 12 * 256
                bdsl2[pp].wr("sync")
                v_dd = None
                for w6 in range(6):
                    v_dd = dma("sync",
                               AP(dslab, po + w6 * 256,
                                  [[4 * 12 * 256, 128], [6 * 256, 2], [1, 256]]),
                               AP(ag_out, 2 * 256 * I_FFN + (2 * pc) * SEC
                                  + w6 * 128 * DM + og * 256,
                                  [[DM, 128], [SEC, 2], [1, 256]]),
                               cont=(w6 > 0))
                bdsl2[pp].wdone(v_dd)
                bdsl2[pp].rd("tensor")
                for jj in range(12):
                    j = pc * 12 + jj
                    lc = po + jj * 256
                    for bb in range(4):
                        b = sh * 4 + bb
                        op("tensor", lambda t, j=j, jj=jj, bb=bb, lc=lc: t.matmul(
                            banks[2 + bb][:, 0:256],
                            fuse[:, j * 512 + bb * 128:j * 512 + (bb + 1) * 128],
                            dslab[:, lc:lc + 256], start=(j == 0), stop=(j == NIC - 1)))
                v_pc = mark("tensor")
                bdsl2[pp].rdone(v_pc)
            v_dmm2 = ("p_t", B.sems["p_t"][1])
            for bb in range(4):
                b = sh * 4 + bb
                bbuf[2 + bb].wdone(v_dmm2)
                bdss2[dp_].rd("vector")
                boa.wr("vector")
                bbuf[2 + bb].rd("vector")
                B.sop("vector", lambda v, bb=bb, dp_=dp_: v.tensor_mul(
                    oacc[:, :], banks[2 + bb][:, 0:256],
                    ds_sl[:, dp_ * 256:(dp_ + 1) * 256]))
                bx.wr("vector")
                v_oadd = B.sop("vector", lambda v, b=b, og=og: v.tensor_add(
                    x_tm[:, b * DM + og * 256:b * DM + (og + 1) * 256],
                    x_tm[:, b * DM + og * 256:b * DM + (og + 1) * 256],
                    oacc[:, :]))
                bbuf[2 + bb].rdone(v_oadd)
                boa.wdone(v_oadd)
                bdss2[dp_].rdone(v_oadd)
                bx.wdone(v_oadd)
        bx.rd("sync")
        for bb in range(4):
            b = sh * 4 + bb
            s = B.sems["o_d"]
            s[1] += 16
            h_, v_ = s[0], s[1]
            B.Q["sync"].append(lambda E, b=b, h_=h_: E.dma_start(
                out=out_h[b * 128:(b + 1) * 128, :],
                in_=x_tm[:, b * DM:(b + 1) * DM]).then_inc(h_, 16))
            n_out += 16
        bx.rdone(("o_d", B.sems["o_d"][1]))

    B.raw("sync", lambda E, v=n_out: E.wait_ge(B.sems["o_d"][0], v))

    with nc.Block() as block:
        @block.sync
        def _(E):
            for fn in B.Q["sync"]:
                fn(E)

        @block.tensor
        def _(E):
            for fn in B.Q["tensor"]:
                fn(E)

        @block.vector
        def _(E):
            for fn in B.Q["vector"]:
                fn(E)

        @block.scalar
        def _(E):
            for fn in B.Q["scalar"]:
                fn(E)

        @block.gpsimd
        def _(E):
            for fn in B.Q["gpsimd"]:
                fn(E)

    stC.close()
    B.st.close()
    return nc


def kernel(**inputs):
    gi = lambda k: np.asarray(inputs[k])
    x = gi("L0_attn_hidden_states").reshape(S_TOT, DM)
    wq = gi("L0_attn_Wq"); bq = gi("L0_attn_bq")
    wk = gi("L0_attn_Wk"); bk = gi("L0_attn_bk")
    wv = gi("L0_attn_Wv"); bv = gi("L0_attn_bv")
    wg = gi("L0_attn_Wg"); bg = gi("L0_attn_bg")
    wmb = gi("L0_attn_Wmb"); bmb = gi("L0_attn_bmb")
    mem = gi("L0_attn_memory"); nrm = gi("L0_attn_norm")
    gw = gi("L0_ffn_gate_w"); uw = gi("L0_ffn_up_w"); dw = gi("L0_ffn_down_w")
    gsc = gi("L0_ffn_gate_scale"); usc = gi("L0_ffn_up_scale")
    dsc = gi("L0_ffn_down_scale")

    wq_pre = np.ascontiguousarray(
        wq.T.reshape(NI, 128, NH, 128).transpose(2, 1, 0, 3)).reshape(
        NH, 128, NI * 128)
    wk_pre = np.ascontiguousarray(
        wk.T.reshape(NI, 128, NKV, 128).transpose(2, 1, 0, 3)).reshape(
        NKV, 128, NI * 128)
    wg_pre = np.ascontiguousarray(
        wg.T.reshape(NI, 128, 4, 512).transpose(2, 1, 0, 3)).reshape(
        4, 128, NI * 512)
    wmbv_pre = np.ascontiguousarray(np.concatenate([
        wmb.T.reshape(NI, 128, NKV, 512).transpose(2, 1, 0, 3),
        wv.T.reshape(NI, 128, NKV, 128).transpose(2, 1, 0, 3)],
        axis=3)).reshape(NKV, 128, NI * 640)
    memnorm = np.concatenate([mem, nrm[:, :, None]], axis=2)
    memnorm_f = np.ascontiguousarray(memnorm.transpose(1, 0, 2)).reshape(F, NH * 129)
    mn_pre = np.ascontiguousarray(
        memnorm_f.reshape(NF, 128, 4, 2, 2, 129).transpose(
            2, 1, 3, 0, 4, 5)).reshape(NKV, 128, 2 * NF * 258)
    mn_seq = np.ascontiguousarray(
        memnorm_f.reshape(NF, 128, NH * 129).transpose(1, 0, 2)).reshape(
        128, NF * 2064)

    common = dict(wqT=wq_pre, wkT=wk_pre, wgT=wg_pre, wmbv=wmbv_pre,
                  bq=bq, bk=bk, bv=bv, bg=bg, bmb=bmb,
                  mn_pre=mn_pre, mn_seq=mn_seq, gate_scale=gsc, up_scale=usc,
                  down_scale=dsc)

    in_maps = []
    for c in range(NCORES):
        wslice = np.concatenate([
            gw[c * 256:(c + 1) * 256, :].reshape(-1),
            uw[c * 256:(c + 1) * 256, :].reshape(-1),
            dw[c * 768:(c + 1) * 768, :].reshape(-1),
        ])
        m = dict(common)
        m["x"] = np.ascontiguousarray(x[c * S:(c + 1) * S, :])
        m["wslice"] = np.ascontiguousarray(wslice)
        in_maps.append(m)

    nc = build_nc()
    res = run_bass_kernel_spmd(nc, in_maps, core_ids=list(range(NCORES)))
    out = np.concatenate([np.asarray(r["out"]) for r in res.results], axis=0)
    return out.reshape(1, S_TOT, DM).astype(np.float32, copy=False)
